# version 05
# baseline (speedup 1.0000x reference)
import numpy as np
import ml_dtypes
import concourse.bass as bass
import concourse.mybir as mybir
from concourse.bass_utils import run_bass_kernel_spmd
from concourse.bass import IndirectOffsetOnAxis
from concourse.alu_op_type import AluOpType as Op
from concourse.hw_specs import get_activation_tables
from contextlib import ExitStack

AF = mybir.ActivationFunctionType
AX = mybir.AxisListType
F32 = mybir.dt.float32
BF16 = mybir.dt.bfloat16
I32 = mybir.dt.int32
U32 = mybir.dt.uint32

NCORES = 8
TTOK = 2048
NT = 16
DM = 1024
CH = 8
NH = 16
DHD = 64
NP = 8
FF = 4096
NE = 8
NFG = 4
NFC = 8
CAP = 1024
NST = CAP // 128
NTG = (CAP + 511) // 512
NSLOT = NE * CAP
S_CONST = float((64.0 ** 0.5) ** (-0.25))
EPS = 1e-5

ENG_SEM = {'tensor': 'ST', 'vector': 'SV', 'scalar': 'SA', 'gpsimd': 'SG'}
SELF_CHAIN = ('vector', 'scalar')


class Opr:
    __slots__ = ('eng', 'build', 'deps', 'dma', 'marked', 'semval', 'dmaval')

    def __init__(self, eng, build, deps, dma):
        self.eng = eng
        self.build = build
        self.deps = [d for d in deps if d is not None]
        self.dma = dma
        self.marked = False
        self.semval = None
        self.dmaval = None


class Sched:
    def __init__(self):
        self.ops = []
        self.last = {}

    def add(self, eng, build, deps=(), dma=None):
        o = Opr(eng, build, deps, dma)
        if eng in SELF_CHAIN and dma is None:
            p = self.last.get(eng)
            if p is not None:
                o.deps.append(p)
            self.last[eng] = o
        self.ops.append(o)
        return o

    def finalize(self):
        dma_counts = {}
        for o in self.ops:
            if o.dma is not None:
                dma_counts[o.dma] = dma_counts.get(o.dma, 0) + 1
                o.dmaval = dma_counts[o.dma] if o.dma in ('CC', 'CC2') else 16 * dma_counts[o.dma]
        for o in self.ops:
            for d in o.deps:
                if d.dma is None:
                    d.marked = True
        cum = {}
        for o in self.ops:
            if o.dma is None and o.marked:
                cum[o.eng] = cum.get(o.eng, 0) + 1
                o.semval = cum[o.eng]

    def emit(self, eng_name, eng, sems):
        last_wait = {}
        for o in self.ops:
            if o.eng != eng_name:
                continue
            for d in o.deps:
                if d.dma is not None:
                    tgt_sem, tgt_val = d.dma, d.dmaval
                else:
                    tgt_sem, tgt_val = ENG_SEM[d.eng], d.semval
                    if tgt_val is None:
                        continue
                if tgt_val > last_wait.get(tgt_sem, 0):
                    eng.wait_ge(sems[tgt_sem], tgt_val)
                    last_wait[tgt_sem] = tgt_val
            inst = o.build(eng) if o.build is not None else None
            if inst is not None:
                if o.dma in ('CC', 'CC2'):
                    inst.then_inc(sems[o.dma])
                elif o.dma is not None:
                    inst.then_inc(sems[o.dma], 16)
                elif o.marked:
                    inst.then_inc(sems[ENG_SEM[o.eng]], 1)


def build_nc(n_experts=NE, tap=False):
    nc = bass.Bass("TRN2", target_bir_lowering=False, debug=False)

    x_d = nc.dram_tensor("x", [TTOK, DM], F32, kind="ExternalInput")
    wq_d = nc.dram_tensor("wq", [DM, DM], F32, kind="ExternalInput")
    bq_d = nc.dram_tensor("bq", [1, DM], F32, kind="ExternalInput")
    wv_d = nc.dram_tensor("wv", [DM, DM], F32, kind="ExternalInput")
    bv_d = nc.dram_tensor("bv", [1, DM], F32, kind="ExternalInput")
    wo_d = nc.dram_tensor("wo", [DM, DM], F32, kind="ExternalInput")
    bo_d = nc.dram_tensor("bo", [1, DM], F32, kind="ExternalInput")
    rt_d = nc.dram_tensor("rt", [DM, NE], F32, kind="ExternalInput")
    tri_d = nc.dram_tensor("tri", [128, 128], BF16, kind="ExternalInput")
    kron_d = nc.dram_tensor("kron", [128, 128], BF16, kind="ExternalInput")
    gw_d = nc.dram_tensor("gw", [NE, DM, FF], F32, kind="ExternalInput")
    uw_d = nc.dram_tensor("uw", [NE, DM, FF], F32, kind="ExternalInput")
    dw_d = nc.dram_tensor("dw", [NE, FF, DM], F32, kind="ExternalInput")
    y_d = nc.dram_tensor("y", [TTOK, DM], F32, kind="ExternalOutput")
    cc_in = nc.dram_tensor("cc_in", [130, 512], F32)
    cc_out = nc.dram_tensor("cc_out", [130, 512], F32)
    cc_bout = nc.dram_tensor("cc_bout", [2, 512], F32)
    xn_dram = nc.dram_tensor("xn_dram", [TTOK + 128, DM], BF16)
    acc_dram = nc.dram_tensor("acc_dram", [TTOK + 128, DM], F32)
    gl_d = nc.dram_tensor("gl", [NSLOT + 128, 1], U32)
    wl_d = nc.dram_tensor("wl", [NSLOT + 128, 1], F32)
    gl_o = wl_o = dbg_o = None
    if tap == 'mini':
        dbgs_o = nc.dram_tensor("dbgs_o", [128, 8], F32, kind="ExternalOutput")
        dbgr_o = nc.dram_tensor("dbgr_o", [8, 128], F32, kind="ExternalOutput")
    if tap is True:
        gl_o = nc.dram_tensor("gl_o", [NSLOT + 128, 1], U32, kind="ExternalOutput")
        wl_o = nc.dram_tensor("wl_o", [NSLOT + 128, 1], F32, kind="ExternalOutput")
        dbg_o = nc.dram_tensor("dbg_o", [2, 128], F32, kind="ExternalOutput")
        dbgs_o = nc.dram_tensor("dbgs_o", [128, 8], F32, kind="ExternalOutput")
        dbgr_o = nc.dram_tensor("dbgr_o", [8, 128], F32, kind="ExternalOutput")
        xnf_o = nc.dram_tensor("xnf_o", [128, CH, TTOK], BF16, kind="ExternalOutput")
        qfh_o = nc.dram_tensor("qfh_o", [128, CH, TTOK], BF16, kind="ExternalOutput")
        ysum_o = nc.dram_tensor("ysum_o", [128, NST, DM], F32, kind="ExternalOutput")

    act_tables = list(get_activation_tables(nc.m.arch).keys())
    IDX_EXPLN = act_tables.index("natural_log_exp_and_others")
    IDX_SILU = act_tables.index("silu_and_others")

    with ExitStack() as es:
        ec = es.enter_context

        xn_f = ec(nc.sbuf_tensor("xn_f", [128, CH, TTOK], BF16))
        qf_h = ec(nc.sbuf_tensor("qf_h", [128, CH, TTOK], BF16))
        acc = ec(nc.sbuf_tensor("acc", [128, 2, DM], F32))
        ysum = ec(nc.sbuf_tensor("ysum", [128, NST, DM], F32))
        ys = ec(nc.sbuf_tensor("ys", [128, 2, DM], F32))
        wb0 = ec(nc.sbuf_tensor("wb0", [128, CH, DM], BF16))
        wb1 = ec(nc.sbuf_tensor("wb1", [128, CH, DM], BF16))
        wb2 = ec(nc.sbuf_tensor("wb2", [128, CH, DM], BF16))
        rt_b = ec(nc.sbuf_tensor("rt_b", [128, CH, NE], BF16))
        tri_b = ec(nc.sbuf_tensor("tri_b", [128, 128], BF16))
        kron_b = ec(nc.sbuf_tensor("kron_b", [128, 128], BF16))
        bqb = ec(nc.sbuf_tensor("bqb", [1, DM], BF16))
        bvb = ec(nc.sbuf_tensor("bvb", [1, DM], BF16))
        bob = ec(nc.sbuf_tensor("bob", [1, DM], BF16))
        xt = ec(nc.sbuf_tensor("xt", [128, 2, DM], F32))
        xnt = ec(nc.sbuf_tensor("xnt", [128, DM], BF16))
        qt = ec(nc.sbuf_tensor("qt", [128, DM], BF16))
        e2t = ec(nc.sbuf_tensor("e2t", [128, DM], BF16))
        vt = ec(nc.sbuf_tensor("vt", [128, DM], BF16))
        csb = ec(nc.sbuf_tensor("csb", [128, 512], F32))
        ssb = ec(nc.sbuf_tensor("ssb", [1, DM], F32))
        sr8 = ec(nc.sbuf_tensor("sr8", [8, 128], F32))
        scol = ec(nc.sbuf_tensor("scol", [128, 8], F32))
        bns = ec(nc.sbuf_tensor("bns", [128, 12], F32))
        mv = ec(nc.sbuf_tensor("mv", [128, 2], F32))
        lnv = ec(nc.sbuf_tensor("lnv", [128, 1], F32))
        rstd = ec(nc.sbuf_tensor("rstd", [128, 1], F32))
        ssum = ec(nc.sbuf_tensor("ssum", [128, NH], F32))
        rq = ec(nc.sbuf_tensor("rq", [128, NH], F32))
        rsb = ec(nc.sbuf_tensor("rsb", [128, NE], F32))
        msk1 = ec(nc.sbuf_tensor("msk1", [128, NE], F32))
        msk2 = ec(nc.sbuf_tensor("msk2", [128, NE], F32))
        rmsk = ec(nc.sbuf_tensor("rmsk", [128, NE], F32))
        ctm = ec(nc.sbuf_tensor("ctm", [128, NE], F32))
        m1 = ec(nc.sbuf_tensor("m1", [128, 1], F32))
        m2 = ec(nc.sbuf_tensor("m2", [128, 1], F32))
        dd = ec(nc.sbuf_tensor("dd", [128, 1], F32))
        ew = ec(nc.sbuf_tensor("ew", [128, 1], F32))
        w12 = ec(nc.sbuf_tensor("w12", [128, NT, 2], F32))
        msk1b = ec(nc.sbuf_tensor("msk1b", [128, NT, NE], BF16))
        msk2b = ec(nc.sbuf_tensor("msk2b", [128, NT, NE], BF16))
        msks = ec(nc.sbuf_tensor("msks", [128, NT, NE], BF16))
        P_sb = ec(nc.sbuf_tensor("P_sb", [128, NT, NE], F32))
        bp_sb = ec(nc.sbuf_tensor("bp_sb", [128, NE], F32))
        ovf_sb = ec(nc.sbuf_tensor("ovf_sb", [128, NE], F32))
        sc1 = ec(nc.sbuf_tensor("sc1", [128, NE], F32))
        sc2 = ec(nc.sbuf_tensor("sc2", [128, 6], F32))
        sfin = ec(nc.sbuf_tensor("sfin", [128, 2], F32))
        sfu = ec(nc.sbuf_tensor("sfu", [128, 2], U32))
        eidx = ec(nc.sbuf_tensor("eidx", [128, NE], F32))
        eidx_i = ec(nc.sbuf_tensor("eidx_i", [128, NE], I32))
        pdg = ec(nc.sbuf_tensor("pdg", [128, 1], F32))
        pdg_i = ec(nc.sbuf_tensor("pdg_i", [128, 1], I32))
        tokid = ec(nc.sbuf_tensor("tokid", [128, NT], U32))
        gdef = ec(nc.sbuf_tensor("gdef", [65, 128], U32))
        wdef = ec(nc.sbuf_tensor("wdef", [65, 128], F32))
        idx_sb = ec(nc.sbuf_tensor("idx_sb", [128, 2, NST], U32))
        wcol_sb = ec(nc.sbuf_tensor("wcol_sb", [128, 2, NST], F32))
        t16row = ec(nc.sbuf_tensor("t16row", [1, 128], F32))
        t16b = ec(nc.sbuf_tensor("t16b", [128, 1], BF16))
        bcol = ec(nc.sbuf_tensor("bcol", [128, 1], F32))
        brow = ec(nc.sbuf_tensor("brow", [1, 128], F32))
        ones1 = ec(nc.sbuf_tensor("ones1", [1, 128], BF16))
        ones1f = ec(nc.sbuf_tensor("ones1f", [1, 128], F32))
        ones128 = ec(nc.sbuf_tensor("ones128", [128, 1], BF16))
        identb = ec(nc.sbuf_tensor("identb", [128, 128], BF16))
        identf = ec(nc.sbuf_tensor("identf", [128, 128], F32))
        iint = ec(nc.sbuf_tensor("iint", [128, 128], I32))
        cst_eps = ec(nc.sbuf_tensor("cst_eps", [128, 1], F32))
        nc.const_aps.register(F32, EPS, cst_eps[:, :])

        ps_tp = ec(nc.psum_tensor("ps_tp", [128, 1024], BF16))
        ps_q = ec(nc.psum_tensor("ps_q", [128, 1024], F32))
        ps_v = ec(nc.psum_tensor("ps_v", [128, 1024], F32))
        ps_c = ec(nc.psum_tensor("ps_c", [128, 512], F32))
        ps_s1 = ec(nc.psum_tensor("ps_s1", [128, 512], F32))
        ps_s2 = ec(nc.psum_tensor("ps_s2", [128, 512], F32))

        sems = {}
        for name in ['ST', 'SV', 'SA', 'SG', 'DX', 'DW', 'DN', 'DL', 'DG',
                     'DS', 'DC', 'CC', 'CC2']:
            sems[name] = ec(nc.semaphore(name))

        sc = Sched()
        H = {}

        op_iota = sc.add('gpsimd', lambda g: g.iota(
            iint[:, :], [[1, 128]], base=0, channel_multiplier=-1))
        op_idb = sc.add('vector', lambda v: v.tensor_scalar(
            identb[:, :], iint[:, :], 0.0, None, Op.is_equal), [op_iota])
        op_idf = sc.add('vector', lambda v: v.tensor_scalar(
            identf[:, :], iint[:, :], 0.0, None, Op.is_equal), [op_iota])
        op_o1 = sc.add('vector', lambda v: v.memset(ones1[:, :], 1.0))
        op_o1f = sc.add('vector', lambda v: v.memset(ones1f[:, :], 1.0))
        op_o128 = sc.add('vector', lambda v: v.memset(ones128[:, :], 1.0))
        op_eps = sc.add('vector', lambda v: v.memset(cst_eps[:, :], EPS))
        op_ei = sc.add('gpsimd', lambda g: g.iota(
            eidx_i[:, :], [[1, NE]], base=0, channel_multiplier=0), [op_iota])
        op_eif = sc.add('vector', lambda v: v.tensor_copy(eidx[:, :], eidx_i[:, :]),
                        [op_ei])
        op_pd = sc.add('gpsimd', lambda g: g.iota(
            pdg_i[:, :], [[1, 1]], base=NSLOT, channel_multiplier=1), [op_ei])
        op_pdf = sc.add('vector', lambda v: v.tensor_copy(pdg[:, :], pdg_i[:, :]),
                        [op_pd])
        op_tk = sc.add('gpsimd', lambda g: g.iota(
            tokid[:, :], [[128, NT]], base=0, channel_multiplier=1), [op_pd])
        op_gd = sc.add('gpsimd', lambda g: g.iota(
            gdef[:, :], [[1, 128]], base=TTOK, channel_multiplier=0), [op_tk])
        op_wd = sc.add('gpsimd', lambda g: g.memset(wdef[:, :], 0.0), [op_gd])
        dl0 = sc.add('gpsimd', lambda g: g.dma_start(
            out=gl_d.ap().rearrange("(a b) o -> a (b o)", b=128), in_=gdef[:, :]),
            [op_gd], dma='DL')
        dl1 = sc.add('gpsimd', lambda g: g.dma_start(
            out=wl_d.ap().rearrange("(a b) o -> a (b o)", b=128), in_=wdef[:, :]),
            [op_wd], dma='DL')
        sc.add('scalar', lambda a: a.add_instruction(mybir.InstLoadActFuncSet(
            name=nc.get_next_instruction_name(), act_func_set_id=IDX_EXPLN,
            ins=[], outs=[])))

        def wdma(dst, src_ap, deps=()):
            return sc.add('gpsimd',
                          lambda g, d=dst, s=src_ap: g.dma_start(out=d, in_=s),
                          deps, dma='DW')

        op_wq = wdma(wb0[:, :, :], wq_d.ap().rearrange("(c p) o -> p c o", p=128))
        op_wv = wdma(wb1[:, :, :], wv_d.ap().rearrange("(c p) o -> p c o", p=128))
        op_wo = wdma(wb2[:, :, :], wo_d.ap().rearrange("(c p) o -> p c o", p=128))
        op_bq = wdma(bqb[:, :], bq_d.ap())
        op_bv = wdma(bvb[:, :], bv_d.ap())
        op_bo = wdma(bob[:, :], bo_d.ap())
        op_rt = wdma(rt_b[:, :, :], rt_d.ap().rearrange("(c p) e -> p c e", p=128))
        op_tri = wdma(tri_b[:, :], tri_d.ap())
        op_kr = wdma(kron_b[:, :], kron_d.ap())
        op_z0 = sc.add('vector', lambda v: v.memset(xnt[:, :], 0.0))
        dn0 = sc.add('sync', lambda s: s.dma_start(
            out=xn_dram[TTOK:TTOK + 128, :], in_=xnt[:, :]), [op_z0], dma='DN')

        def xdma(tt, deps):
            return sc.add('sync', lambda s, a=tt * 128, sl=tt % 2: s.dma_start(
                out=xt[:, sl, :], in_=x_d[a:a + 128, :]), deps, dma='DX')

        xops = [None] * NT
        xops[0] = xdma(0, [])
        xops[1] = xdma(1, [])

        prev = {}
        for tt in range(NT):
            sl = tt % 2
            ttw = slice(tt * 128, (tt + 1) * 128)
            dx = xops[tt]
            b0 = sc.add('vector', lambda v, sl=sl: v.bn_stats(
                bns[:, 0:6], xt[:, sl, 0:512]), [dx])
            b1 = sc.add('vector', lambda v, sl=sl: v.bn_stats(
                bns[:, 6:12], xt[:, sl, 512:1024]), [dx])
            ag = sc.add('vector', lambda v: v.bn_aggr(mv[:, 0:2], bns[:, 0:12]),
                        [b1, prev.get('ln')])
            ln = sc.add('scalar', lambda a: a.activation(
                lnv[:, :], mv[:, 1:2], AF.Ln, bias=EPS), [ag, op_eps])
            prev['ln'] = ln
            rs = sc.add('scalar', lambda a: a.activation(
                rstd[:, :], lnv[:, :], AF.Exp, scale=-0.5), [ln, prev.get('xn')])
            xn = sc.add('vector', lambda v, sl=sl: v.tensor_scalar(
                xnt[:, :], xt[:, sl, :], mv[:, 0:1], rstd[:, :], Op.subtract, Op.mult),
                [rs, prev.get('xnT'), prev.get('dnd'), dn0 if tt == 0 else None])
            prev['xn'] = xn
            dnd = sc.add('sync', lambda s, a=tt * 128: s.dma_start(
                out=xn_dram[a:a + 128, :], in_=xnt[:, :]), [xn], dma='DN')
            prev['dnd'] = dnd
            tpl = None
            for k in range(CH):
                tpl = sc.add('tensor', lambda t, k=k: t.transpose(
                    ps_tp[:, k * 128:(k + 1) * 128], xnt[:, k * 128:(k + 1) * 128],
                    identb[:, :]),
                    [xn, op_idb, prev.get('qfev')] if k == 0 else [])
            prev['xnT'] = tpl
            xnev = sc.add('scalar', lambda a, ttw=ttw: a.copy(
                xn_f[:, :, ttw], ps_tp.ap().rearrange("p (c f) -> p c f", c=CH)),
                [tpl])
            qmm = None
            for hf in range(2):
                cols = slice(hf * 512, (hf + 1) * 512)
                qmm = sc.add('tensor', lambda t, cols=cols: t.matmul(
                    ps_q[:, cols], ones1[:, :], bqb[:, cols], start=True, stop=False),
                    [xnev, op_wq, op_bq, op_o1, prev.get('eq')] if hf == 0 else [])
                for c in range(CH):
                    qmm = sc.add('tensor', lambda t, c=c, cols=cols, ttw=ttw: t.matmul(
                        ps_q[:, cols], xn_f[:, c, ttw], wb0[:, c, cols],
                        start=False, stop=(c == CH - 1)))
            H['qmm%d' % tt] = qmm
            vmm = None
            for hf in range(2):
                cols = slice(hf * 512, (hf + 1) * 512)
                vmm = sc.add('tensor', lambda t, cols=cols: t.matmul(
                    ps_v[:, cols], ones1[:, :], bvb[:, cols], start=True, stop=False),
                    [xnev, op_wv, op_bv, op_o1, prev.get('vev')] if hf == 0 else [])
                for c in range(CH):
                    vmm = sc.add('tensor', lambda t, c=c, cols=cols, ttw=ttw: t.matmul(
                        ps_v[:, cols], xn_f[:, c, ttw], wb1[:, c, cols],
                        start=False, stop=(c == CH - 1)))
            H['vmm%d' % tt] = vmm
            eq = sc.add('scalar', lambda a, sl=sl: a.activation(
                xt[:, sl, :], ps_q[:, :], AF.Exp), [qmm, xn])
            prev['eq'] = eq
            rd = sc.add('vector', lambda v, sl=sl: v.tensor_reduce(
                ssum[:, :], xt.ap()[:, sl, :].rearrange("p (h d) -> p h d", h=NH),
                AX.X, Op.add), [eq])
            rc = sc.add('vector', lambda v: v.reciprocal(rq[:, :], ssum[:, :]), [rd])
            qm = None
            for hh in range(NH):
                cs = slice(hh * DHD, (hh + 1) * DHD)
                qm = sc.add('vector', lambda v, cs=cs, hh=hh, sl=sl: v.tensor_scalar(
                    qt[:, cs], xt[:, sl, cs], rq[:, hh:hh + 1], S_CONST, Op.mult, Op.mult),
                    [rc, prev.get('e2'), prev.get('qT')] if hh == 0 else [])
            H['qm%d' % tt] = qm
            e2 = sc.add('scalar', lambda a: a.activation(
                e2t[:, :], qt[:, :], AF.Exp), [qm, prev.get('CS')])
            prev['e2'] = e2
            tpl2 = None
            for k in range(CH):
                tpl2 = sc.add('tensor', lambda t, k=k: t.transpose(
                    ps_tp[:, k * 128:(k + 1) * 128], qt[:, k * 128:(k + 1) * 128],
                    identb[:, :]), [qm, xnev] if k == 0 else [])
            prev['qT'] = tpl2
            qfev = sc.add('scalar', lambda a, ttw=ttw: a.copy(
                qf_h[:, :, ttw], ps_tp.ap().rearrange("p (c f) -> p c f", c=CH)),
                [tpl2])
            prev['qfev'] = qfev
            vev = sc.add('scalar', lambda a: a.copy(vt[:, :], ps_v[:, :]),
                         [vmm, prev.get('CS')])
            prev['vev'] = vev
            cmm = None
            for p in range(NP):
                for j in range(2):
                    hh = 2 * p + j
                    first = (tt == 0 and p == 0 and j == 0)
                    stopc = (tt == NT - 1 and p == NP - 1 and j == 1)
                    cmm = sc.add('tensor', lambda t, p=p, j=j, hh=hh, first=first, stopc=stopc: t.matmul(
                        ps_c[64 * j:64 * (j + 1), 64 * p:64 * (p + 1)],
                        e2t[:, hh * 64:(hh + 1) * 64], vt[:, hh * 64:(hh + 1) * 64],
                        start=first, stop=stopc, skip_group_check=True),
                        [e2, vev] if (p == 0 and j == 0) else [])
            H['Cmm%d' % tt] = cmm
            smm = sc.add('tensor', lambda t, st=(tt == 0), sp=(tt == NT - 1): t.matmul(
                ps_s1[0:1, :], ones128[:, :], e2t[:, 0:512],
                start=st, stop=sp, skip_group_check=True),
                [e2, op_o128] if tt == 0 else [e2])
            smm2 = sc.add('tensor', lambda t, st=(tt == 0), sp=(tt == NT - 1): t.matmul(
                ps_s2[0:1, :], ones128[:, :], e2t[:, 512:1024],
                start=st, stop=sp, skip_group_check=True), [])
            prev['CS'] = smm2
            H['Smm%d' % tt] = smm2
            if tt + 2 < NT:
                xops[tt + 2] = xdma(tt + 2, [qm])

        cev = sc.add('vector', lambda v: v.tensor_copy(csb[:, :], ps_c[:, :]),
                     [H['Cmm%d' % (NT - 1)]])
        sev1 = sc.add('scalar', lambda a: a.copy(ssb[:, 0:512], ps_s1[0:1, :]),
                      [H['Smm%d' % (NT - 1)]])
        sev2 = sc.add('scalar', lambda a: a.copy(ssb[:, 512:1024], ps_s2[0:1, :]),
                      [H['Smm%d' % (NT - 1)]])
        d1 = sc.add('gpsimd', lambda g: g.dma_start(out=cc_in[0:128, :], in_=csb[:, :]),
                    [cev], dma='DC')
        d2 = sc.add('gpsimd', lambda g: g.dma_start(out=cc_in[128:129, :], in_=ssb[:, 0:512]),
                    [sev1], dma='DC')
        d3 = sc.add('gpsimd', lambda g: g.dma_start(out=cc_in[129:130, :], in_=ssb[:, 512:1024]),
                    [sev2], dma='DC')
        ccop = sc.add('gpsimd', lambda g: g.collective_compute(
            "AllReduce", Op.add,
            replica_groups=[[0, 1, 2, 3], [4, 5, 6, 7]],
            ins=[cc_in.ap().opt()], outs=[cc_out.ap().opt()]),
            [d1, d2, d3], dma='CC')
        ccb = sc.add('gpsimd', lambda g: g.collective_compute(
            "AllReduce", Op.add,
            replica_groups=[[0, 1, 2, 3], [4, 5, 6, 7]],
            ins=[cc_in.ap()[128:130, :].opt()], outs=[cc_bout.ap().opt()]),
            [ccop], dma='CC2')
        db1 = sc.add('gpsimd', lambda g: g.dma_start(out=csb[:, :], in_=cc_out[0:128, :]),
                     [ccb, d1], dma='DC')
        db2 = sc.add('gpsimd', lambda g: g.dma_start(
            out=sr8[:, :],
            in_=cc_out.ap()[128:130, :].rearrange("a (c q) -> (a c) q", q=128)),
            [ccb], dma='DC')

        prevR = {}
        ex = None
        tmm = None
        for tt in range(NT):
            ttw = slice(tt * 128, (tt + 1) * 128)
            psR = ps_s2 if tt % 2 == 0 else ps_c
            rmm = None
            for c in range(CH):
                rmm = sc.add('tensor', lambda t, c=c, ttw=ttw, psR=psR: t.matmul(
                    psR[:, 0:NE], xn_f[:, c, ttw], rt_b[:, c, :],
                    start=(c == 0), stop=(c == CH - 1)),
                    ([sev2, cev, op_rt, prevR.get(tt % 2)]) if c == 0 else [])
            rcp = sc.add('vector', lambda v, psR=psR: v.tensor_copy(
                rsb[:, :], psR[:, 0:NE]), [rmm])
            prevR[tt % 2] = rcp
            mx1 = sc.add('vector', lambda v: v.tensor_reduce(
                m1[:, :], rsb[:, :], AX.X, Op.max), [rcp])
            mk1 = sc.add('vector', lambda v: v.tensor_scalar(
                msk1[:, :], rsb[:, :], m1[:, 0:1], None, Op.is_ge), [mx1])
            rmv = sc.add('vector', lambda v: v.scalar_tensor_tensor(
                rmsk[:, :], msk1[:, :], -1e30, rsb[:, :], Op.mult, Op.add), [mk1])
            mx2 = sc.add('vector', lambda v: v.tensor_reduce(
                m2[:, :], rmsk[:, :], AX.X, Op.max), [rmv])
            sub = sc.add('vector', lambda v: v.tensor_tensor(
                dd[:, :], m1[:, :], m2[:, :], Op.subtract), [mx2])
            ex = sc.add('scalar', lambda a: a.activation(
                ew[:, :], dd[:, :], AF.Exp, scale=-1.0), [sub])
            ad_ = sc.add('vector', lambda v: v.tensor_scalar(
                ctm[:, 0:1], ew[:, :], 1.0, None, Op.add), [ex])
            rc1 = sc.add('vector', lambda v, tt=tt: v.reciprocal(
                w12[:, tt, 0:1], ctm[:, 0:1]), [ad_])
            w2o = sc.add('vector', lambda v, tt=tt: v.tensor_scalar(
                w12[:, tt, 1:2], w12[:, tt, 0:1], -1.0, 1.0, Op.mult, Op.add), [rc1])
            mk2 = sc.add('vector', lambda v: v.tensor_scalar(
                msk2[:, :], rmsk[:, :], m2[:, 0:1], None, Op.is_ge), [mx2])
            sc.add('vector', lambda v, tt=tt: v.tensor_copy(
                msk1b[:, tt, :], msk1[:, :]), [mk1])
            sc.add('vector', lambda v, tt=tt: v.tensor_copy(
                msk2b[:, tt, :], msk2[:, :]), [mk2])
            cpm = sc.add('vector', lambda v, tt=tt: v.tensor_tensor(
                msks[:, tt, :], msk1[:, :], msk2[:, :], Op.add), [mk1, mk2])
            psPh = ps_q[:, 0:NE] if tt % 2 == 0 else ps_q[:, 512:512 + NE]
            pmm = sc.add('tensor', lambda t, tt=tt, psPh=psPh: t.matmul(
                psPh, tri_b[:, :], msks[:, tt, :], start=True, stop=True,
                skip_group_check=True),
                [cpm, op_tri] + ([prev.get('eq')] if tt < 2 else [prevR.get('P%d' % (tt % 2))]))
            pev = sc.add('vector', lambda v, tt=tt, psPh=psPh: v.tensor_copy(
                P_sb[:, tt, :], psPh), [pmm])
            prevR['P%d' % (tt % 2)] = pev
            prevR['cpm'] = cpm
        H['w2oL'] = w2o
        tmm = sc.add('tensor', lambda t: t.matmul(
            ps_v[0:1, 0:128], ones128[:, :],
            msks.ap().rearrange("p a e -> p (a e)"),
            start=True, stop=True, skip_group_check=True),
            [prevR['cpm'], prev.get('vev'), op_o128])
        sc.add('scalar', lambda a: a.add_instruction(mybir.InstLoadActFuncSet(
            name=nc.get_next_instruction_name(), act_func_set_id=IDX_SILU,
            ins=[], outs=[])), [ex])

        tev = sc.add('scalar', lambda a: a.copy(t16row[:, :], ps_v[0:1, 0:128]),
                     [tmm])
        ttr = sc.add('tensor', lambda t: t.transpose(
            ps_v[:, 128:129], t16row[:, :], identf[0:1, 0:1]), [tev, op_idf])
        tbc = sc.add('vector', lambda v: v.tensor_copy(t16b[:, :], ps_v[:, 128:129]),
                     [ttr])
        kmm = sc.add('tensor', lambda t: t.matmul(
            ps_v[:, 132:133], kron_b[:, :], t16b[:, :], start=True, stop=True,
            skip_group_check=True), [tbc, op_kr])
        bev = sc.add('vector', lambda v: v.tensor_copy(bcol[:, :], ps_v[:, 132:133]),
                     [kmm])
        btr = sc.add('tensor', lambda t: t.transpose(
            ps_v[0:1, 136:264], bcol[:, :], identf[:, :]), [bev])
        brv = sc.add('scalar', lambda a: a.copy(brow[:, :], ps_v[0:1, 136:264]),
                     [btr])

        stp = sc.add('tensor', lambda t: t.transpose(
            ps_s1[:, 0:8], sr8[:, :], identf[0:8, 0:8]),
            [db2, op_idf, sev1])
        srq = sc.add('vector', lambda v: v.reciprocal(scol[:, :], ps_s1[:, 0:8]), [stp])
        cbl = None
        for p in range(NP):
            cs = slice(64 * p, 64 * (p + 1))
            cbl = sc.add('vector', lambda v, p=p, cs=cs: v.tensor_scalar(
                e2t[:, cs], csb[:, cs], scol[:, p:p + 1], S_CONST, Op.mult, Op.mult),
                [srq, db1, prev['e2']] if p == 0 else [])
        xre = [None] * NT
        xre[0] = sc.add('sync', lambda s: s.dma_start(
            out=xt[:, 0, :], in_=x_d[0:128, :]), [H['qm%d' % (NT - 2)]], dma='DX')
        xre[1] = sc.add('sync', lambda s: s.dma_start(
            out=xt[:, 1, :], in_=x_d[128:256, :]), [H['qm%d' % (NT - 1)]], dma='DX')
        prev3 = {}
        prevS = {}
        prev_sck = {}
        for tt in range(NT):
            amm = None
            for p in range(NP):
                for j in range(2):
                    rows = slice(64 * j, 64 * (j + 1))
                    amm = sc.add('tensor', lambda t, p=p, rows=rows, tt=tt: t.matmul(
                        ps_q[rows, 128 * p:128 * (p + 1)],
                        e2t[rows, 64 * p:64 * (p + 1)],
                        qf_h.ap()[rows, p, tt * 128:(tt + 1) * 128],
                        start=True, stop=True, skip_group_check=True),
                        [cbl, prev3.get('aev'),
                         prevR.get('P0'), prevR.get('P1')] if (p == 0 and j == 0) else [])
            aev = sc.add('scalar', lambda a: a.copy(vt[:, :], ps_q[:, :]),
                         [amm, prev3.get('omm'), prev.get('CS')])
            prev3['aev'] = aev
            omm = None
            for hf in range(2):
                cols = slice(hf * 512, (hf + 1) * 512)
                omm = sc.add('tensor', lambda t, cols=cols: t.matmul(
                    ps_v[:, cols], ones1[:, :], bob[:, cols], start=True, stop=False),
                    [aev, op_wo, op_bo, prev3.get('acc0') or brv] if hf == 0 else [])
                for c in range(CH):
                    omm = sc.add('tensor', lambda t, c=c, cols=cols: t.matmul(
                        ps_v[:, cols], vt[:, c * 128:(c + 1) * 128], wb2[:, c, cols],
                        start=False, stop=(c == CH - 1)))
            prev3['omm'] = omm
            a0 = sc.add('vector', lambda v, sl=tt % 2: v.scalar_tensor_tensor(
                acc[:, sl, :], ps_v[:, :], 1.0, xt[:, sl, :], Op.mult, Op.add),
                [omm, xre[tt], prev3.get('ad%d' % (tt % 2))])
            prev3['acc0'] = a0
            adp = sc.add('gpsimd', lambda g, a=tt * 128, sl=tt % 2: g.dma_start(
                out=acc_dram[a:a + 128, :], in_=acc[:, sl, :]), [a0], dma='DS')
            prev3['ad%d' % (tt % 2)] = adp
            H['acc_last'] = adp
            if tt + 2 < NT:
                xre[tt + 2] = sc.add('sync', lambda s, a=(tt + 2) * 128, sl=tt % 2: s.dma_start(
                    out=xt[:, sl, :], in_=x_d[a:a + 128, :]), [a0], dma='DX')
            psS = ps_c[:, 0:NE] if tt % 2 == 0 else ps_s2[:, 0:NE]
            bmm = sc.add('tensor', lambda t, tt=tt, psS=psS: t.matmul(
                psS, ones1f[:, :], brow[0:1, 8 * tt:8 * tt + 8],
                start=True, stop=True, skip_group_check=True),
                [brv, op_o1f, prevS.get(tt % 2), prevR.get((tt % 2) ^ 1) if tt < 2 else None])
            bpo = sc.add('vector', lambda v, tt=tt, psS=psS: v.tensor_tensor(
                bp_sb[:, :], P_sb[:, tt, :], psS, Op.add), [bmm])
            prevS[tt % 2] = bpo
            ov = sc.add('vector', lambda v: v.tensor_scalar(
                ovf_sb[:, :], bp_sb[:, :], CAP - 0.5, None, Op.is_ge), [bpo])
            for k, mkb in ((0, msk1b), (1, msk2b)):
                t1 = sc.add('vector', lambda v, tt=tt, mkb=mkb: v.tensor_tensor(
                    sc1[:, :], mkb[:, tt, :], bp_sb[:, :], Op.mult), [bpo, ov])
                r1 = sc.add('vector', lambda v, k=k: v.tensor_reduce(
                    sc2[:, 3 * k:3 * k + 1], sc1[:, :], AX.X, Op.add), [t1])
                t2 = sc.add('vector', lambda v, tt=tt, mkb=mkb: v.tensor_tensor(
                    sc1[:, :], mkb[:, tt, :], eidx[:, :], Op.mult), [r1, op_eif])
                r2 = sc.add('vector', lambda v, k=k: v.tensor_reduce(
                    sc2[:, 3 * k + 1:3 * k + 2], sc1[:, :], AX.X, Op.add), [t2])
                t3 = sc.add('vector', lambda v, tt=tt, mkb=mkb: v.tensor_tensor(
                    sc1[:, :], mkb[:, tt, :], ovf_sb[:, :], Op.mult), [r2])
                r3 = sc.add('vector', lambda v, k=k: v.tensor_reduce(
                    sc2[:, 3 * k + 2:3 * k + 3], sc1[:, :], AX.X, Op.add), [t3])
                s0 = sc.add('vector', lambda v, k=k: v.scalar_tensor_tensor(
                    sfin[:, k:k + 1], sc2[:, 3 * k + 1:3 * k + 2], float(CAP),
                    sc2[:, 3 * k:3 * k + 1], Op.mult, Op.add), [r3])
                s1_ = sc.add('vector', lambda v, k=k: v.tensor_tensor(
                    sc1[:, 0:1], pdg[:, :], sfin[:, k:k + 1], Op.subtract),
                    [s0, op_pdf])
                s2_ = sc.add('vector', lambda v, k=k: v.tensor_tensor(
                    sc1[:, 1:2], sc1[:, 0:1], sc2[:, 3 * k + 2:3 * k + 3], Op.mult),
                    [s1_])
                s3_ = sc.add('vector', lambda v, k=k: v.tensor_tensor(
                    sfin[:, k:k + 1], sfin[:, k:k + 1], sc1[:, 1:2], Op.add), [s2_])
                s4_ = sc.add('vector', lambda v, k=k: v.tensor_copy(
                    sfu[:, k:k + 1], sfin[:, k:k + 1]),
                    [s3_, prev_sck.get(k)])
                g1 = sc.add('gpsimd', lambda g, tt=tt, k=k: g.indirect_dma_start(
                    out=gl_d[:, :],
                    out_offset=IndirectOffsetOnAxis(ap=sfu[:, k:k + 1], axis=0),
                    in_=tokid[:, tt:tt + 1], in_offset=None),
                    [s4_, dl0, op_tk], dma='DL')
                g2 = sc.add('gpsimd', lambda g, tt=tt, k=k: g.indirect_dma_start(
                    out=wl_d[:, :],
                    out_offset=IndirectOffsetOnAxis(ap=sfu[:, k:k + 1], axis=0),
                    in_=w12[:, tt, k:k + 1], in_offset=None),
                    [s4_, H['w2oL'], dl1], dma='DL')
                prev_sck[k] = g2
        H['slots'] = prev_sck[1]

        prev_gu_last = {'wb0': H['qmm%d' % (NT - 1)], 'wb1': H['vmm%d' % (NT - 1)]}
        prev_dw = prev3['omm']
        prev_hm = None
        prev_silu = None
        prev_ds = H['acc_last']
        prev_ysum = {}
        prev_tp = prev['qfev']
        tp_done = {}
        scat_done = {}
        gu_done = {}
        if tap is True:
            prev_ds = sc.add('gpsimd', lambda g: g.dma_start(
                out=gl_o[:, :], in_=gl_d[:, :]), [H['slots'], prev_ds], dma='DS')
            prev_ds = sc.add('gpsimd', lambda g: g.dma_start(
                out=wl_o[:, :], in_=wl_d[:, :]), [prev_ds], dma='DS')
            prev_ds = sc.add('gpsimd', lambda g: g.dma_start(
                out=dbg_o[0:1, :], in_=t16row[:, :]), [brv, prev_ds], dma='DS')
            prev_ds = sc.add('gpsimd', lambda g: g.dma_start(
                out=dbg_o[1:2, :], in_=brow[:, :]), [prev_ds], dma='DS')
            prev_ds = sc.add('gpsimd', lambda g: g.dma_start(
                out=dbgs_o[:, :], in_=scol[:, :]), [srq, prev_ds], dma='DS')
            prev_ds = sc.add('gpsimd', lambda g: g.dma_start(
                out=dbgr_o[:, :], in_=sr8[:, :]), [prev_ds], dma='DS')
        for e in range(n_experts):
            he = e % 2
            xgoff = he * 1024
            il = None
            for j in range(NST):
                il = sc.add('gpsimd', lambda g, e=e, j=j, he=he: g.dma_start(
                    out=idx_sb[:, he, j:j + 1],
                    in_=gl_d[e * CAP + 128 * j:e * CAP + 128 * (j + 1), :]),
                    [H['slots'], scat_done.get(e - 2)] if j == 0 else [], dma='DG')
                il = sc.add('gpsimd', lambda g, e=e, j=j, he=he: g.dma_start(
                    out=wcol_sb[:, he, j:j + 1],
                    in_=wl_d[e * CAP + 128 * j:e * CAP + 128 * (j + 1), :]),
                    [], dma='DG')
            H['wload%d' % e] = il
            gth = None
            for j in range(NST):
                gth = sc.add('gpsimd', lambda g, j=j, he=he: g.indirect_dma_start(
                    out=qf_h[:, j, 1024:2048], out_offset=None,
                    in_=xn_dram[:, :],
                    in_offset=IndirectOffsetOnAxis(ap=idx_sb[:, he, j:j + 1], axis=0)),
                    ([il, prev.get('dnd'), dn0, prev3['aev'],
                      tp_done.get(e - 1)]) if j == 0 else [], dma='DG')
            tpv = None
            for j in range(NST):
                tpj = None
                for c in range(CH):
                    tpj = sc.add('tensor', lambda t, j=j, c=c: t.transpose(
                        ps_tp[:, c * 128:(c + 1) * 128],
                        qf_h[:, j, 1024 + 128 * c:1024 + 128 * (c + 1)],
                        identb[:, :]),
                        ([gth] if j == 0 else []) + [prev_tp] if c == 0 else [])
                tpv = sc.add('scalar', lambda a, j=j, xgoff=xgoff: a.copy(
                    xn_f[:, :, xgoff + 128 * j:xgoff + 128 * (j + 1)],
                    ps_tp.ap().rearrange("p (c f) -> p c f", c=CH)), [tpj])
                prev_tp = tpv
            tp_done[e] = tpv
            prev_tp = tpv
            for fg in range(NFG):
                k = e * NFG + fg
                fgw = slice(fg * 1024, (fg + 1) * 1024)
                gdma = sc.add('gpsimd', lambda g, e=e, fgw=fgw: g.dma_start(
                    out=wb0[:, :, :],
                    in_=gw_d.ap()[e, :, fgw].rearrange("(c p) f -> p c f", p=128)),
                    [prev_gu_last['wb0']], dma='DW')
                udma = sc.add('gpsimd', lambda g, e=e, fgw=fgw: g.dma_start(
                    out=wb1[:, :, :],
                    in_=uw_d.ap()[e, :, fgw].rearrange("(c p) f -> p c f", p=128)),
                    [prev_gu_last['wb1']], dma='DW')
                ddma = sc.add('gpsimd', lambda g, e=e, fgw=fgw: g.dma_start(
                    out=wb2[:, :, :],
                    in_=dw_d.ap()[e, fgw, :].rearrange("(f p) o -> p f o", p=128)),
                    [prev_dw], dma='DW')
                gu = None
                for fc in range(NFC):
                    fcs = slice(fc * 128, (fc + 1) * 128)
                    for tg in range(NTG):
                        tgw = min(512, CAP - tg * 512)
                        gmm = None
                        for c in range(CH):
                            gdeps = []
                            if c == 0:
                                gdeps = [gdma, tpv, prev_silu]
                                if k == 0:
                                    gdeps += [prevS.get(0), prevR.get(1)]
                            gmm = sc.add('tensor', lambda t, c=c, fcs=fcs, tg=tg, tgw=tgw, xgoff=xgoff: t.matmul(
                                ps_c[:, 0:tgw], wb0[:, c, fcs],
                                xn_f[:, c, xgoff + tg * 512:xgoff + tg * 512 + tgw],
                                start=(c == 0), stop=(c == CH - 1)), gdeps)
                        umm = None
                        for c in range(CH):
                            udeps = []
                            if c == 0:
                                udeps = [udma, prev_hm]
                                if k == 0:
                                    udeps.append(srq)
                            umm = sc.add('tensor', lambda t, c=c, fcs=fcs, tg=tg, tgw=tgw, xgoff=xgoff: t.matmul(
                                ps_s1[:, 0:tgw], wb1[:, c, fcs],
                                xn_f[:, c, xgoff + tg * 512:xgoff + tg * 512 + tgw],
                                start=(c == 0), stop=(c == CH - 1)), udeps)
                        sil = sc.add('scalar', lambda a, tgw=tgw: a.activation(
                            qt[:, 0:tgw], ps_c[:, 0:tgw], AF.Silu),
                            [gmm, prev_hm] + ([prevS.get(0)] if k == 0 else []))
                        prev_silu = sil
                        hm = sc.add('vector', lambda v, fc=fc, tg=tg, tgw=tgw: v.tensor_tensor(
                            qf_h[:, fc, tg * 512:tg * 512 + tgw], qt[:, 0:tgw],
                            ps_s1[:, 0:tgw], Op.mult),
                            [sil, umm] + ([prev3['aev']] if k == 0 else []))
                        prev_hm = hm
                        gu = hm
                prev_gu_last = {'wb0': gu, 'wb1': gu}
                gu_done[e] = gu
                dn = None
                for st in range(NST):
                    psy = ps_q if st % 2 == 0 else ps_v
                    dmm = None
                    for hf in range(2):
                        cols = slice(hf * 512, (hf + 1) * 512)
                        for fc in range(NFC):
                            ddeps = []
                            if hf == 0 and fc == 0:
                                ddeps = [ddma, gu, prev_ysum.get(st % 2)]
                                if k == 0:
                                    ddeps += [prev3['aev'], prev3['acc0'],
                                              prevR.get('P0'), prevR.get('P1')]
                            dmm = sc.add('tensor', lambda t, fc=fc, cols=cols, st=st, psy=psy: t.matmul(
                                psy[:, cols], qf_h[:, fc, st * 128:(st + 1) * 128],
                                wb2[:, fc, cols],
                                start=(fc == 0), stop=(fc == NFC - 1)), ddeps)
                    if fg == 0:
                        yv = sc.add('vector', lambda v, st=st, psy=psy: v.tensor_scalar(
                            ysum[:, st, :], psy[:, :], 1.0, None, Op.mult),
                            [dmm, scat_done.get('st%d' % st)])
                    else:
                        yv = sc.add('vector', lambda v, st=st, psy=psy: v.scalar_tensor_tensor(
                            ysum[:, st, :], psy[:, :], 1.0, ysum[:, st, :],
                            Op.mult, Op.add), [dmm])
                    prev_ysum[st % 2] = yv
                    H['yv%d_%d' % (k, st)] = yv
                    dn = dmm
                prev_dw = dn
            scl = None
            for st in range(NST):
                se = sc.add('scalar', lambda a, st=st, he=he: a.activation(
                    ys[:, st % 2, :], ysum[:, st, :], AF.Copy,
                    scale=wcol_sb[:, he, st:st + 1]),
                    [H['yv%d_%d' % (e * NFG + NFG - 1, st)], H['wload%d' % e],
                     scat_done.get('ys%d' % (st % 2))])
                sct = sc.add('gpsimd', lambda g, st=st, he=he: g.indirect_dma_start(
                    out=acc_dram[:, :],
                    out_offset=IndirectOffsetOnAxis(ap=idx_sb[:, he, st:st + 1], axis=0),
                    in_=ys[:, st % 2, :], in_offset=None,
                    compute_op=Op.add),
                    [se, prev_ds], dma='DS')
                prev_ds = sct
                scat_done['st%d' % st] = sct
                scat_done['ys%d' % (st % 2)] = sct
                scl = sct
            scat_done[e] = scl

        if tap == 'mini':
            prev_ds = sc.add('gpsimd', lambda g: g.dma_start(
                out=dbgs_o[:, :], in_=scol[:, :]), [srq, prev_ds], dma='DS')
            prev_ds = sc.add('gpsimd', lambda g: g.dma_start(
                out=dbgr_o[:, :], in_=sr8[:, :]), [prev_ds], dma='DS')
        if tap is True:
            prev_ds = sc.add('gpsimd', lambda g: g.dma_start(
                out=xnf_o[:, :, :], in_=xn_f[:, :, :]), [prev_ds], dma='DS')
            prev_ds = sc.add('gpsimd', lambda g: g.dma_start(
                out=qfh_o[:, :, :], in_=qf_h[:, :, :]), [prev_ds], dma='DS')
            prev_ds = sc.add('gpsimd', lambda g: g.dma_start(
                out=ysum_o[:, :, :], in_=ysum[:, :, :]), [prev_ds], dma='DS')
        fcp = None
        prev_b = {}
        for tt in range(NT):
            sl = tt % 2
            fin_in = sc.add('gpsimd', lambda g, a=tt * 128, sl=sl: g.dma_start(
                out=ys[:, sl, :], in_=acc_dram[a:a + 128, :]),
                [prev_ds, prev_b.get(sl)], dma='DS')
            fcp = sc.add('gpsimd', lambda g, a=tt * 128, sl=sl: g.dma_start(
                out=y_d[a:a + 128, :], in_=ys[:, sl, :]), [fin_in], dma='DS')
            prev_b[sl] = fcp
        sc.add('sync', None, [fcp])

        sc.finalize()
        ds_total = sum(16 for o in sc.ops if o.dma == 'DS')

        with nc.Block() as block:
            block.sync(lambda eng: (sc.emit('sync', eng, sems),
                                    eng.wait_ge(sems['DS'], ds_total)))
            block.tensor(lambda eng: sc.emit('tensor', eng, sems))
            block.vector(lambda eng: sc.emit('vector', eng, sems))
            block.scalar(lambda eng: sc.emit('scalar', eng, sems))
            block.gpsimd(lambda eng: sc.emit('gpsimd', eng, sems))

    return nc


_NC_CACHE = {}


def _get_nc():
    if 'nc' not in _NC_CACHE:
        _NC_CACHE['nc'] = build_nc()
    return _NC_CACHE['nc']


def kernel(L0_attn_x, L0_attn_Wq, L0_attn_bq, L0_attn_Wk, L0_attn_bk,
           L0_attn_Wv, L0_attn_bv, L0_attn_Wo, L0_attn_bo,
           L0_ffn_router_w, L0_ffn_gate_w, L0_ffn_up_w, L0_ffn_down_w):
    x = np.ascontiguousarray(np.asarray(L0_attn_x, dtype=np.float32).reshape(2 * 8192, DM))
    wq = np.ascontiguousarray(np.asarray(L0_attn_Wq, dtype=np.float32))
    bq = np.ascontiguousarray(np.asarray(L0_attn_bq, dtype=np.float32).reshape(1, DM))
    wv = np.ascontiguousarray(np.asarray(L0_attn_Wv, dtype=np.float32))
    bv = np.ascontiguousarray(np.asarray(L0_attn_bv, dtype=np.float32).reshape(1, DM))
    wo = np.ascontiguousarray(np.asarray(L0_attn_Wo, dtype=np.float32))
    bo = np.ascontiguousarray(np.asarray(L0_attn_bo, dtype=np.float32).reshape(1, DM))
    rt = np.ascontiguousarray(np.asarray(L0_ffn_router_w, dtype=np.float32).T)
    gw = np.ascontiguousarray(np.asarray(L0_ffn_gate_w, dtype=np.float32).transpose(0, 2, 1))
    uw = np.ascontiguousarray(np.asarray(L0_ffn_up_w, dtype=np.float32).transpose(0, 2, 1))
    dw = np.ascontiguousarray(np.asarray(L0_ffn_down_w, dtype=np.float32).transpose(0, 2, 1))
    tri = np.triu(np.ones((128, 128), np.float32), 1).astype(ml_dtypes.bfloat16)
    kron = np.kron(np.triu(np.ones((NT, NT), np.float32), 1),
                   np.eye(NE, dtype=np.float32)).astype(ml_dtypes.bfloat16)

    nc = _get_nc()
    in_maps = []
    for c in range(NCORES):
        in_maps.append({
            'x': np.ascontiguousarray(x[c * TTOK:(c + 1) * TTOK]),
            'wq': wq, 'bq': bq, 'wv': wv, 'bv': bv, 'wo': wo, 'bo': bo,
            'rt': rt, 'tri': tri, 'kron': kron,
            'gw': gw, 'uw': uw, 'dw': dw,
        })
    res = run_bass_kernel_spmd(nc, in_maps, core_ids=list(range(NCORES)))
    out = np.concatenate([r['y'] for r in res.results], axis=0)
    return np.ascontiguousarray(out.reshape(2, 8192, DM).astype(np.float32))


# version 09
# speedup vs baseline: 1.2294x; 1.2294x over previous
import numpy as np
import ml_dtypes
import concourse.bass as bass
import concourse.mybir as mybir
from concourse.bass_utils import run_bass_kernel_spmd
from concourse.bass import IndirectOffsetOnAxis
from concourse.alu_op_type import AluOpType as Op
from concourse.hw_specs import get_activation_tables
from contextlib import ExitStack

AF = mybir.ActivationFunctionType
AX = mybir.AxisListType
F32 = mybir.dt.float32
BF16 = mybir.dt.bfloat16
I32 = mybir.dt.int32
U32 = mybir.dt.uint32

NCORES = 8
TTOK = 2048
NT = 16
DM = 1024
CH = 8
NH = 16
DHD = 64
NP = 8
FF = 4096
NE = 8
NFG = 4
NFC = 8
CAP = 768
NST = CAP // 128
NTG = (CAP + 511) // 512
NSLOT = NE * CAP
S_CONST = float((64.0 ** 0.5) ** (-0.25))
EPS = 1e-5

ENG_SEM = {'tensor': 'ST', 'vector': 'SV', 'scalar': 'SA', 'gpsimd': 'SG'}
SELF_CHAIN = ('vector', 'scalar')


class Opr:
    __slots__ = ('eng', 'build', 'deps', 'dma', 'marked', 'semval', 'dmaval')

    def __init__(self, eng, build, deps, dma):
        self.eng = eng
        self.build = build
        self.deps = [d for d in deps if d is not None]
        self.dma = dma
        self.marked = False
        self.semval = None
        self.dmaval = None


class Sched:
    def __init__(self):
        self.ops = []
        self.last = {}

    def add(self, eng, build, deps=(), dma=None):
        o = Opr(eng, build, deps, dma)
        if eng in SELF_CHAIN and dma is None:
            p = self.last.get(eng)
            if p is not None:
                o.deps.append(p)
            self.last[eng] = o
        self.ops.append(o)
        return o

    def finalize(self):
        dma_counts = {}
        for o in self.ops:
            if o.dma is not None:
                dma_counts[o.dma] = dma_counts.get(o.dma, 0) + 1
                o.dmaval = dma_counts[o.dma] if o.dma in ('CC', 'CC2') else 16 * dma_counts[o.dma]
        for o in self.ops:
            for d in o.deps:
                if d.dma is None:
                    d.marked = True
        cum = {}
        for o in self.ops:
            if o.dma is None and o.marked:
                cum[o.eng] = cum.get(o.eng, 0) + 1
                o.semval = cum[o.eng]

    def emit(self, eng_name, eng, sems):
        last_wait = {}
        for o in self.ops:
            if o.eng != eng_name:
                continue
            for d in o.deps:
                if d.dma is not None:
                    tgt_sem, tgt_val = d.dma, d.dmaval
                else:
                    tgt_sem, tgt_val = ENG_SEM[d.eng], d.semval
                    if tgt_val is None:
                        continue
                if tgt_val > last_wait.get(tgt_sem, 0):
                    eng.wait_ge(sems[tgt_sem], tgt_val)
                    last_wait[tgt_sem] = tgt_val
            inst = o.build(eng) if o.build is not None else None
            if inst is not None:
                if o.dma in ('CC', 'CC2'):
                    inst.then_inc(sems[o.dma])
                elif o.dma is not None:
                    inst.then_inc(sems[o.dma], 16)
                elif o.marked:
                    inst.then_inc(sems[ENG_SEM[o.eng]], 1)


def build_nc(n_experts=NE, tap=False):
    nc = bass.Bass("TRN2", target_bir_lowering=False, debug=False)

    x_d = nc.dram_tensor("x", [TTOK, DM], F32, kind="ExternalInput")
    wq_d = nc.dram_tensor("wq", [DM, DM], F32, kind="ExternalInput")
    bq_d = nc.dram_tensor("bq", [1, DM], F32, kind="ExternalInput")
    wv_d = nc.dram_tensor("wv", [DM, DM], F32, kind="ExternalInput")
    bv_d = nc.dram_tensor("bv", [1, DM], F32, kind="ExternalInput")
    wo_d = nc.dram_tensor("wo", [DM, DM], F32, kind="ExternalInput")
    bo_d = nc.dram_tensor("bo", [1, DM], F32, kind="ExternalInput")
    rt_d = nc.dram_tensor("rt", [DM, NE], F32, kind="ExternalInput")
    tri_d = nc.dram_tensor("tri", [128, 128], BF16, kind="ExternalInput")
    kron_d = nc.dram_tensor("kron", [128, 128], BF16, kind="ExternalInput")
    gw_d = nc.dram_tensor("gw", [NE, DM, FF], F32, kind="ExternalInput")
    uw_d = nc.dram_tensor("uw", [NE, DM, FF], F32, kind="ExternalInput")
    dw_d = nc.dram_tensor("dw", [NE, FF, DM], F32, kind="ExternalInput")
    y_d = nc.dram_tensor("y", [TTOK, DM], F32, kind="ExternalOutput")
    cc_in = nc.dram_tensor("cc_in", [260, 512], F32)
    cc_out = nc.dram_tensor("cc_out", [260, 512], F32)
    xn_dram = nc.dram_tensor("xn_dram", [TTOK + 128, DM], BF16)
    acc_dram = nc.dram_tensor("acc_dram", [TTOK + 128, DM], F32)
    cc_junk = nc.dram_tensor("cc_junk", [32, 1024], F32)
    gl_d = nc.dram_tensor("gl", [NSLOT + 128, 1], U32)
    wl_d = nc.dram_tensor("wl", [NSLOT + 128, 1], F32)
    gl_o = wl_o = dbg_o = None
    if tap == 'mini':
        dbgs_o = nc.dram_tensor("dbgs_o", [128, 8], F32, kind="ExternalOutput")
        dbgr_o = nc.dram_tensor("dbgr_o", [8, 128], F32, kind="ExternalOutput")
    if tap is True:
        gl_o = nc.dram_tensor("gl_o", [NSLOT + 128, 1], U32, kind="ExternalOutput")
        wl_o = nc.dram_tensor("wl_o", [NSLOT + 128, 1], F32, kind="ExternalOutput")
        dbg_o = nc.dram_tensor("dbg_o", [2, 128], F32, kind="ExternalOutput")
        dbgs_o = nc.dram_tensor("dbgs_o", [128, 8], F32, kind="ExternalOutput")
        dbgr_o = nc.dram_tensor("dbgr_o", [8, 128], F32, kind="ExternalOutput")
        xnf_o = nc.dram_tensor("xnf_o", [128, CH, TTOK], BF16, kind="ExternalOutput")
        qfh_o = nc.dram_tensor("qfh_o", [128, CH, TTOK], BF16, kind="ExternalOutput")
        ysum_o = nc.dram_tensor("ysum_o", [128, NST, DM], F32, kind="ExternalOutput")

    act_tables = list(get_activation_tables(nc.m.arch).keys())
    IDX_EXPLN = act_tables.index("natural_log_exp_and_others")
    IDX_SILU = act_tables.index("silu_and_others")

    with ExitStack() as es:
        ec = es.enter_context

        xn_f = ec(nc.sbuf_tensor("xn_f", [128, CH, TTOK], BF16))
        qf_h = ec(nc.sbuf_tensor("qf_h", [128, CH, TTOK], BF16))
        acc = ec(nc.sbuf_tensor("acc", [128, 2, DM], F32))
        ysum = ec(nc.sbuf_tensor("ysum", [128, NST, DM], F32))
        hbuf = ec(nc.sbuf_tensor("hbuf", [128, CH, CAP], BF16))
        gtb = ec(nc.sbuf_tensor("gtb", [128, 2 * NST, DM], BF16))
        wb0 = ec(nc.sbuf_tensor("wb0", [128, CH, DM], BF16))
        wb1 = ec(nc.sbuf_tensor("wb1", [128, CH, DM], BF16))
        wb2 = ec(nc.sbuf_tensor("wb2", [128, CH, DM], BF16))
        rt_b = ec(nc.sbuf_tensor("rt_b", [128, CH, NE], BF16))
        bob = ec(nc.sbuf_tensor("bob", [1, DM], BF16))
        xt = ec(nc.sbuf_tensor("xt", [128, 2, DM], F32))
        qt = ec(nc.sbuf_tensor("qt", [128, DM], BF16))
        e2t = ec(nc.sbuf_tensor("e2t", [128, DM], BF16))
        vt = ec(nc.sbuf_tensor("vt", [128, DM], BF16))
        csb = ec(nc.sbuf_tensor("csb", [128, 512], F32))
        sr8 = ec(nc.sbuf_tensor("sr8", [8, 128], F32))
        scol = ec(nc.sbuf_tensor("scol", [128, 8], F32))
        bns = ec(nc.sbuf_tensor("bns", [128, 12], F32))
        mv = ec(nc.sbuf_tensor("mv", [128, 2], F32))
        lnv = ec(nc.sbuf_tensor("lnv", [128, 1], F32))
        rstd = ec(nc.sbuf_tensor("rstd", [128, 1], F32))
        ssum = ec(nc.sbuf_tensor("ssum", [128, NH], F32))
        rq = ec(nc.sbuf_tensor("rq", [128, NH], F32))
        rsb = ec(nc.sbuf_tensor("rsb", [128, NE], F32))
        msk1 = ec(nc.sbuf_tensor("msk1", [128, NE], F32))
        msk2 = ec(nc.sbuf_tensor("msk2", [128, NE], F32))
        rmsk = ec(nc.sbuf_tensor("rmsk", [128, NE], F32))
        ctm = ec(nc.sbuf_tensor("ctm", [128, NE], F32))
        m1 = ec(nc.sbuf_tensor("m1", [128, 1], F32))
        m2 = ec(nc.sbuf_tensor("m2", [128, 1], F32))
        dd = ec(nc.sbuf_tensor("dd", [128, 1], F32))
        ew = ec(nc.sbuf_tensor("ew", [128, 1], F32))
        w12 = ec(nc.sbuf_tensor("w12", [128, NT, 2], F32))
        msk1b = ec(nc.sbuf_tensor("msk1b", [128, NT, NE], BF16))
        msk2b = ec(nc.sbuf_tensor("msk2b", [128, NT, NE], BF16))
        msks = ec(nc.sbuf_tensor("msks", [128, NT, NE], BF16))
        P_sb = ec(nc.sbuf_tensor("P_sb", [128, NT, NE], F32))
        bp_sb = ec(nc.sbuf_tensor("bp_sb", [128, NE], F32))
        ovf_sb = ec(nc.sbuf_tensor("ovf_sb", [128, NE], F32))
        sc1 = ec(nc.sbuf_tensor("sc1", [128, NE], F32))
        sc2 = ec(nc.sbuf_tensor("sc2", [128, 6], F32))
        sfin = ec(nc.sbuf_tensor("sfin", [128, 2], F32))
        sfu = ec(nc.sbuf_tensor("sfu", [128, 2], U32))
        eidx = ec(nc.sbuf_tensor("eidx", [128, NE], F32))
        pdg = ec(nc.sbuf_tensor("pdg", [128, 1], F32))
        tokid = ec(nc.sbuf_tensor("tokid", [128, NT], U32))
        idx_sb = ec(nc.sbuf_tensor("idx_sb", [128, 2, NST], U32))
        wcol_sb = ec(nc.sbuf_tensor("wcol_sb", [128, 2, NST], F32))
        t16row = ec(nc.sbuf_tensor("t16row", [1, 128], F32))
        t16b = ec(nc.sbuf_tensor("t16b", [128, 1], BF16))
        bcol = ec(nc.sbuf_tensor("bcol", [128, 1], F32))
        brow = ec(nc.sbuf_tensor("brow", [1, 128], F32))
        ones1 = ec(nc.sbuf_tensor("ones1", [1, 128], BF16))
        ones1f = ec(nc.sbuf_tensor("ones1f", [1, 128], F32))
        ones128 = ec(nc.sbuf_tensor("ones128", [128, 1], BF16))
        identb = ec(nc.sbuf_tensor("identb", [128, 128], BF16))
        identf = ec(nc.sbuf_tensor("identf", [128, 128], F32))
        cst_eps = ec(nc.sbuf_tensor("cst_eps", [128, 1], F32))
        nc.const_aps.register(F32, EPS, cst_eps[:, :])

        ps_tp = ec(nc.psum_tensor("ps_tp", [128, 1024], BF16))
        ps_q = ec(nc.psum_tensor("ps_q", [128, 1024], F32))
        ps_v = ec(nc.psum_tensor("ps_v", [128, 1024], F32))
        ps_c = ec(nc.psum_tensor("ps_c", [128, 512], F32))
        ps_s1 = ec(nc.psum_tensor("ps_s1", [128, 512], F32))
        ps_s2 = ec(nc.psum_tensor("ps_s2", [128, 512], F32))

        sems = {}
        for name in ['ST', 'SV', 'SA', 'SG', 'DX', 'DW', 'DN', 'DL', 'DG',
                     'DS', 'DC', 'CC', 'CC2']:
            sems[name] = ec(nc.semaphore(name))

        sc = Sched()
        H = {}

        op_iota = sc.add('gpsimd', lambda g: g.iota(
            identf[:, :], [[1, 128]], base=0, channel_multiplier=-1,
            allow_small_or_imprecise_dtypes=True))
        op_idb = sc.add('vector', lambda v: v.tensor_scalar(
            identb[:, :], identf[:, :], 0.0, None, Op.is_equal), [op_iota])
        op_idf = sc.add('vector', lambda v: v.tensor_scalar(
            identf[:, :], identf[:, :], 0.0, None, Op.is_equal), [op_idb])
        op_o1 = sc.add('vector', lambda v: v.memset(ones1[:, :], 1.0))
        op_o1f = sc.add('vector', lambda v: v.memset(ones1f[:, :], 1.0))
        op_o128 = sc.add('vector', lambda v: v.memset(ones128[:, :], 1.0))
        op_eps = sc.add('vector', lambda v: v.memset(cst_eps[:, :], EPS))
        op_eif = sc.add('gpsimd', lambda g: g.iota(
            eidx[:, :], [[1, NE]], base=0, channel_multiplier=0,
            allow_small_or_imprecise_dtypes=True), [op_iota])
        op_pdf = sc.add('gpsimd', lambda g: g.iota(
            pdg[:, :], [[1, 1]], base=NSLOT, channel_multiplier=1,
            allow_small_or_imprecise_dtypes=True), [op_eif])
        op_tk = sc.add('gpsimd', lambda g: g.iota(
            tokid[:, :], [[128, NT]], base=0, channel_multiplier=1), [op_pdf])
        op_gd = sc.add('gpsimd', lambda g: g.iota(
            ysum.ap()[0:(NSLOT + 128) // 128, 0, 0:128], [[1, 128]], base=TTOK, channel_multiplier=0,
            allow_small_or_imprecise_dtypes=True), [op_tk])
        op_wd = sc.add('gpsimd', lambda g: g.memset(ysum[0:(NSLOT + 128) // 128, 0, 128:256], 0.0),
                       [op_gd])
        dl0 = sc.add('gpsimd', lambda g: g.dma_start(
            out=gl_d.ap().rearrange("(a b) o -> a (b o)", b=128),
            in_=ysum[0:(NSLOT + 128) // 128, 0, 0:128]), [op_gd], dma='DL')
        dl1 = sc.add('gpsimd', lambda g: g.dma_start(
            out=wl_d.ap().rearrange("(a b) o -> a (b o)", b=128),
            in_=ysum[0:(NSLOT + 128) // 128, 0, 128:256]), [op_wd], dma='DL')
        sc.add('scalar', lambda a: a.add_instruction(mybir.InstLoadActFuncSet(
            name=nc.get_next_instruction_name(), act_func_set_id=IDX_EXPLN,
            ins=[], outs=[])))

        ccw = sc.add('gpsimd', lambda g: g.collective_compute(
            "AllReduce", Op.add,
            replica_groups=[[0, 1, 2, 3], [4, 5, 6, 7]],
            ins=[acc_dram.ap()[0:32, :].opt()], outs=[cc_junk.ap().opt()]),
            [op_iota], dma='CC')

        def wdma(dst, src_ap, deps=()):
            return sc.add('gpsimd',
                          lambda g, d=dst, s=src_ap: g.dma_start(out=d, in_=s),
                          deps, dma='DW')

        op_wq = wdma(wb0[:, :, :], wq_d.ap().rearrange("(c p) o -> p c o", p=128))
        op_wv = wdma(wb1[:, :, :], wv_d.ap().rearrange("(c p) o -> p c o", p=128))
        op_bq = wdma(gtb[0:1, 0, :], bq_d.ap())
        op_bv = wdma(gtb[0:1, 1, :], bv_d.ap())
        op_bo = wdma(bob[:, :], bo_d.ap())
        op_rt = wdma(rt_b[:, :, :], rt_d.ap().rearrange("(c p) e -> p c e", p=128))
        op_tri = wdma(gtb[:, 2, 0:128], tri_d.ap())
        op_kr = wdma(gtb[:, 3, 0:128], kron_d.ap())
        op_z0 = sc.add('vector', lambda v: v.memset(hbuf.ap().rearrange("p a b -> p (a b)")[:, 0:1024], 0.0))
        dn0 = sc.add('sync', lambda s: s.dma_start(
            out=xn_dram[TTOK:TTOK + 128, :], in_=hbuf.ap().rearrange("p a b -> p (a b)")[:, 0:1024]), [op_z0], dma='DN')

        def xdma(tt, deps):
            return sc.add('sync', lambda s, a=tt * 128, sl=tt % 2: s.dma_start(
                out=xt[:, sl, :], in_=x_d[a:a + 128, :]), deps, dma='DX')

        xops = [None] * NT
        xops[0] = xdma(0, [])
        xops[1] = xdma(1, [])

        prev = {}
        for tt in range(NT):
            sl = tt % 2
            ttw = slice(tt * 128, (tt + 1) * 128)
            dx = xops[tt]
            b0 = sc.add('vector', lambda v, sl=sl: v.bn_stats(
                bns[:, 0:6], xt[:, sl, 0:512]), [dx])
            b1 = sc.add('vector', lambda v, sl=sl: v.bn_stats(
                bns[:, 6:12], xt[:, sl, 512:1024]), [dx])
            ag = sc.add('vector', lambda v: v.bn_aggr(mv[:, 0:2], bns[:, 0:12]),
                        [b1, prev.get('ln')])
            ln = sc.add('scalar', lambda a: a.activation(
                lnv[:, :], mv[:, 1:2], AF.Ln, bias=EPS), [ag, op_eps])
            prev['ln'] = ln
            rs = sc.add('scalar', lambda a: a.activation(
                rstd[:, :], lnv[:, :], AF.Exp, scale=-0.5), [ln, prev.get('xn')])
            xn = sc.add('vector', lambda v, sl=sl: v.tensor_scalar(
                hbuf.ap().rearrange("p a b -> p (a b)")[:, 0:1024], xt[:, sl, :], mv[:, 0:1], rstd[:, :], Op.subtract, Op.mult),
                [rs, prev.get('xnT'), prev.get('dnd'), dn0 if tt == 0 else None])
            prev['xn'] = xn
            dnd = sc.add('sync', lambda s, a=tt * 128: s.dma_start(
                out=xn_dram[a:a + 128, :], in_=hbuf.ap().rearrange("p a b -> p (a b)")[:, 0:1024]), [xn], dma='DN')
            prev['dnd'] = dnd
            tpl = None
            for k in range(CH):
                tpl = sc.add('tensor', lambda t, k=k: t.transpose(
                    ps_tp[:, k * 128:(k + 1) * 128], hbuf.ap().rearrange("p a b -> p (a b)")[:, k * 128:(k + 1) * 128],
                    identb[:, :]),
                    [xn, op_idb, prev.get('qfev')] if k == 0 else [])
            prev['xnT'] = tpl
            xnev = sc.add('scalar', lambda a, ttw=ttw: a.copy(
                xn_f[:, :, ttw], ps_tp.ap().rearrange("p (c f) -> p c f", c=CH)),
                [tpl])
            qmm = None
            for hf in range(2):
                cols = slice(hf * 512, (hf + 1) * 512)
                qmm = sc.add('tensor', lambda t, cols=cols: t.matmul(
                    ps_q[:, cols], ones1[:, :], gtb[0:1, 0, cols], start=True, stop=False),
                    [xnev, op_wq, op_bq, op_o1, prev.get('eq')] if hf == 0 else [])
                for c in range(CH):
                    qmm = sc.add('tensor', lambda t, c=c, cols=cols, ttw=ttw: t.matmul(
                        ps_q[:, cols], xn_f[:, c, ttw], wb0[:, c, cols],
                        start=False, stop=(c == CH - 1)))
            H['qmm%d' % tt] = qmm
            vmm = None
            for hf in range(2):
                cols = slice(hf * 512, (hf + 1) * 512)
                vmm = sc.add('tensor', lambda t, cols=cols: t.matmul(
                    ps_v[:, cols], ones1[:, :], gtb[0:1, 1, cols], start=True, stop=False),
                    [xnev, op_wv, op_bv, op_o1, prev.get('vev')] if hf == 0 else [])
                for c in range(CH):
                    vmm = sc.add('tensor', lambda t, c=c, cols=cols, ttw=ttw: t.matmul(
                        ps_v[:, cols], xn_f[:, c, ttw], wb1[:, c, cols],
                        start=False, stop=(c == CH - 1)))
            H['vmm%d' % tt] = vmm
            eq = sc.add('scalar', lambda a, sl=sl: a.activation(
                xt[:, sl, :], ps_q[:, :], AF.Exp), [qmm, xn])
            prev['eq'] = eq
            rd = sc.add('vector', lambda v, sl=sl: v.tensor_reduce(
                ssum[:, :], xt.ap()[:, sl, :].rearrange("p (h d) -> p h d", h=NH),
                AX.X, Op.add), [eq])
            rc = sc.add('vector', lambda v: v.reciprocal(rq[:, :], ssum[:, :]), [rd])
            qm = None
            for hh in range(NH):
                cs = slice(hh * DHD, (hh + 1) * DHD)
                qm = sc.add('vector', lambda v, cs=cs, hh=hh, sl=sl: v.tensor_scalar(
                    qt[:, cs], xt[:, sl, cs], rq[:, hh:hh + 1], S_CONST, Op.mult, Op.mult),
                    [rc, prev.get('e2'), prev.get('qT')] if hh == 0 else [])
            H['qm%d' % tt] = qm
            e2 = sc.add('scalar', lambda a: a.activation(
                e2t[:, :], qt[:, :], AF.Exp), [qm, prev.get('CS')])
            prev['e2'] = e2
            tpl2 = None
            for k in range(CH):
                tpl2 = sc.add('tensor', lambda t, k=k: t.transpose(
                    ps_tp[:, k * 128:(k + 1) * 128], qt[:, k * 128:(k + 1) * 128],
                    identb[:, :]), [qm, xnev] if k == 0 else [])
            prev['qT'] = tpl2
            qfev = sc.add('scalar', lambda a, ttw=ttw: a.copy(
                qf_h[:, :, ttw], ps_tp.ap().rearrange("p (c f) -> p c f", c=CH)),
                [tpl2])
            prev['qfev'] = qfev
            vev = sc.add('scalar', lambda a: a.copy(vt[:, :], ps_v[:, :]),
                         [vmm, prev.get('CS')])
            prev['vev'] = vev
            cmm = None
            for p in range(NP):
                for j in range(2):
                    hh = 2 * p + j
                    first = (tt == 0 and p == 0 and j == 0)
                    stopc = (tt == NT - 1 and p == NP - 1 and j == 1)
                    cmm = sc.add('tensor', lambda t, p=p, j=j, hh=hh, first=first, stopc=stopc: t.matmul(
                        ps_c[64 * j:64 * (j + 1), 64 * p:64 * (p + 1)],
                        e2t[:, hh * 64:(hh + 1) * 64], vt[:, hh * 64:(hh + 1) * 64],
                        start=first, stop=stopc, skip_group_check=True),
                        [e2, vev] if (p == 0 and j == 0) else [])
            H['Cmm%d' % tt] = cmm
            smm = sc.add('tensor', lambda t, st=(tt == 0), sp=(tt == NT - 1): t.matmul(
                ps_s1[0:1, :], ones128[:, :], e2t[:, 0:512],
                start=st, stop=sp, skip_group_check=True),
                [e2, op_o128] if tt == 0 else [e2])
            smm2 = sc.add('tensor', lambda t, st=(tt == 0), sp=(tt == NT - 1): t.matmul(
                ps_s2[0:1, :], ones128[:, :], e2t[:, 512:1024],
                start=st, stop=sp, skip_group_check=True), [])
            prev['CS'] = smm2
            H['Smm%d' % tt] = smm2
            if tt + 2 < NT:
                xops[tt + 2] = xdma(tt + 2, [qm])

        cev = sc.add('vector', lambda v: v.tensor_copy(csb[:, :], ps_c[:, :]),
                     [H['Cmm%d' % (NT - 1)]])
        sev1 = sc.add('scalar', lambda a: a.copy(acc[0:1, 0, 0:512], ps_s1[0:1, :]),
                      [H['Smm%d' % (NT - 1)]])
        sev2 = sc.add('scalar', lambda a: a.copy(acc[0:1, 0, 512:1024], ps_s2[0:1, :]),
                      [H['Smm%d' % (NT - 1)]])
        d1 = sc.add('gpsimd', lambda g: g.dma_start(out=cc_in[0:128, :], in_=csb[:, :]),
                    [cev], dma='DC')
        d1b = sc.add('gpsimd', lambda g: g.dma_start(out=cc_in[130:258, :], in_=csb[:, :]),
                     [cev], dma='DC')
        d1c = sc.add('gpsimd', lambda g: g.dma_start(out=cc_in[258:260, :], in_=csb[0:2, :]),
                     [cev], dma='DC')
        d2 = sc.add('gpsimd', lambda g: g.dma_start(out=cc_in[128:129, :], in_=acc[0:1, 0, 0:512]),
                    [sev1], dma='DC')
        d3 = sc.add('gpsimd', lambda g: g.dma_start(out=cc_in[129:130, :], in_=acc[0:1, 0, 512:1024]),
                    [sev2], dma='DC')
        ccop = sc.add('gpsimd', lambda g: g.collective_compute(
            "AllReduce", Op.add,
            replica_groups=[[0, 1, 2, 3], [4, 5, 6, 7]],
            ins=[cc_in.ap().opt()], outs=[cc_out.ap().opt()]),
            [d1, d1b, d1c, d2, d3, ccw], dma='CC2')
        ccb0 = sc.add('gpsimd', lambda g: g.collective_compute(
            "AllReduce", Op.add,
            replica_groups=[[0, 1, 2, 3], [4, 5, 6, 7]],
            ins=[cc_in.ap().opt()], outs=[cc_out.ap().opt()]),
            [ccop], dma='CC')
        ccb = sc.add('gpsimd', lambda g: g.collective_compute(
            "AllReduce", Op.add,
            replica_groups=[[0, 1, 2, 3], [4, 5, 6, 7]],
            ins=[cc_in.ap().opt()], outs=[cc_out.ap().opt()]),
            [ccb0], dma='CC2')

        prevR = {}
        ex = None
        tmm = None
        for tt in range(NT):
            ttw = slice(tt * 128, (tt + 1) * 128)
            psR = ps_s2 if tt % 2 == 0 else ps_c
            rmm = None
            for c in range(CH):
                rmm = sc.add('tensor', lambda t, c=c, ttw=ttw, psR=psR: t.matmul(
                    psR[:, 0:NE], xn_f[:, c, ttw], rt_b[:, c, :],
                    start=(c == 0), stop=(c == CH - 1)),
                    ([sev2, cev, op_rt, prevR.get(tt % 2)]) if c == 0 else [])
            rcp = sc.add('vector', lambda v, psR=psR: v.tensor_copy(
                rsb[:, :], psR[:, 0:NE]), [rmm])
            prevR[tt % 2] = rcp
            mx1 = sc.add('vector', lambda v: v.tensor_reduce(
                m1[:, :], rsb[:, :], AX.X, Op.max), [rcp])
            mk1 = sc.add('vector', lambda v: v.tensor_scalar(
                msk1[:, :], rsb[:, :], m1[:, 0:1], None, Op.is_ge), [mx1])
            rmv = sc.add('vector', lambda v: v.scalar_tensor_tensor(
                rmsk[:, :], msk1[:, :], -1e30, rsb[:, :], Op.mult, Op.add), [mk1])
            mx2 = sc.add('vector', lambda v: v.tensor_reduce(
                m2[:, :], rmsk[:, :], AX.X, Op.max), [rmv])
            sub = sc.add('vector', lambda v: v.tensor_tensor(
                dd[:, :], m1[:, :], m2[:, :], Op.subtract), [mx2])
            ex = sc.add('scalar', lambda a: a.activation(
                ew[:, :], dd[:, :], AF.Exp, scale=-1.0), [sub])
            ad_ = sc.add('vector', lambda v: v.tensor_scalar(
                ctm[:, 0:1], ew[:, :], 1.0, None, Op.add), [ex])
            rc1 = sc.add('vector', lambda v, tt=tt: v.reciprocal(
                w12[:, tt, 0:1], ctm[:, 0:1]), [ad_])
            w2o = sc.add('vector', lambda v, tt=tt: v.tensor_scalar(
                w12[:, tt, 1:2], w12[:, tt, 0:1], -1.0, 1.0, Op.mult, Op.add), [rc1])
            mk2 = sc.add('vector', lambda v: v.tensor_scalar(
                msk2[:, :], rmsk[:, :], m2[:, 0:1], None, Op.is_ge), [mx2])
            sc.add('vector', lambda v, tt=tt: v.tensor_copy(
                msk1b[:, tt, :], msk1[:, :]), [mk1])
            sc.add('vector', lambda v, tt=tt: v.tensor_copy(
                msk2b[:, tt, :], msk2[:, :]), [mk2])
            cpm = sc.add('vector', lambda v, tt=tt: v.tensor_tensor(
                msks[:, tt, :], msk1[:, :], msk2[:, :], Op.add), [mk1, mk2])
            psPh = ps_q[:, 0:NE] if tt % 2 == 0 else ps_q[:, 512:512 + NE]
            pmm = sc.add('tensor', lambda t, tt=tt, psPh=psPh: t.matmul(
                psPh, gtb[:, 2, 0:128], msks[:, tt, :], start=True, stop=True,
                skip_group_check=True),
                [cpm, op_tri] + ([prev.get('eq')] if tt < 2 else [prevR.get('P%d' % (tt % 2))]))
            pev = sc.add('vector', lambda v, tt=tt, psPh=psPh: v.tensor_copy(
                P_sb[:, tt, :], psPh), [pmm])
            prevR['P%d' % (tt % 2)] = pev
            prevR['cpm'] = cpm
        H['w2oL'] = w2o
        tmm = sc.add('tensor', lambda t: t.matmul(
            ps_v[0:1, 0:128], ones128[:, :],
            msks.ap().rearrange("p a e -> p (a e)"),
            start=True, stop=True, skip_group_check=True),
            [prevR['cpm'], prev.get('vev'), op_o128])
        sc.add('scalar', lambda a: a.add_instruction(mybir.InstLoadActFuncSet(
            name=nc.get_next_instruction_name(), act_func_set_id=IDX_SILU,
            ins=[], outs=[])), [ex])

        tev = sc.add('scalar', lambda a: a.copy(t16row[:, :], ps_v[0:1, 0:128]),
                     [tmm])
        ttr = sc.add('tensor', lambda t: t.transpose(
            ps_v[:, 128:129], t16row[:, :], identf[0:1, 0:1]), [tev, op_idf])
        tbc = sc.add('vector', lambda v: v.tensor_copy(t16b[:, :], ps_v[:, 128:129]),
                     [ttr])
        kmm = sc.add('tensor', lambda t: t.matmul(
            ps_v[:, 132:133], gtb[:, 3, 0:128], t16b[:, :], start=True, stop=True,
            skip_group_check=True), [tbc, op_kr])
        bev = sc.add('vector', lambda v: v.tensor_copy(bcol[:, :], ps_v[:, 132:133]),
                     [kmm])
        btr = sc.add('tensor', lambda t: t.transpose(
            ps_v[0:1, 136:264], bcol[:, :], identf[:, :]), [bev])
        brv = sc.add('scalar', lambda a: a.copy(brow[:, :], ps_v[0:1, 136:264]),
                     [btr])

        prevS = {}
        prev_sck = {}
        for tt in range(NT):
            psS = ps_c[:, 0:NE] if tt % 2 == 0 else ps_s2[:, 0:NE]
            bmm = sc.add('tensor', lambda t, tt=tt, psS=psS: t.matmul(
                psS, ones1f[:, :], brow[0:1, 8 * tt:8 * tt + 8],
                start=True, stop=True, skip_group_check=True),
                [brv, op_o1f, prevS.get(tt % 2),
                 prevR.get((tt % 2) ^ 1) if tt < 2 else None])
            bpo = sc.add('vector', lambda v, tt=tt, psS=psS: v.tensor_tensor(
                bp_sb[:, :], P_sb[:, tt, :], psS, Op.add), [bmm])
            prevS[tt % 2] = bpo
            ov = sc.add('vector', lambda v: v.tensor_scalar(
                ovf_sb[:, :], bp_sb[:, :], CAP - 0.5, None, Op.is_ge), [bpo])
            for k, mkb in ((0, msk1b), (1, msk2b)):
                t1 = sc.add('vector', lambda v, tt=tt, mkb=mkb: v.tensor_tensor(
                    sc1[:, :], mkb[:, tt, :], bp_sb[:, :], Op.mult), [bpo, ov])
                r1 = sc.add('vector', lambda v, k=k: v.tensor_reduce(
                    sc2[:, 3 * k:3 * k + 1], sc1[:, :], AX.X, Op.add), [t1])
                t2 = sc.add('vector', lambda v, tt=tt, mkb=mkb: v.tensor_tensor(
                    sc1[:, :], mkb[:, tt, :], eidx[:, :], Op.mult), [r1, op_eif])
                r2 = sc.add('vector', lambda v, k=k: v.tensor_reduce(
                    sc2[:, 3 * k + 1:3 * k + 2], sc1[:, :], AX.X, Op.add), [t2])
                t3 = sc.add('vector', lambda v, tt=tt, mkb=mkb: v.tensor_tensor(
                    sc1[:, :], mkb[:, tt, :], ovf_sb[:, :], Op.mult), [r2])
                r3 = sc.add('vector', lambda v, k=k: v.tensor_reduce(
                    sc2[:, 3 * k + 2:3 * k + 3], sc1[:, :], AX.X, Op.add), [t3])
                s0 = sc.add('vector', lambda v, k=k: v.scalar_tensor_tensor(
                    sfin[:, k:k + 1], sc2[:, 3 * k + 1:3 * k + 2], float(CAP),
                    sc2[:, 3 * k:3 * k + 1], Op.mult, Op.add), [r3])
                s1_ = sc.add('vector', lambda v, k=k: v.tensor_tensor(
                    sc1[:, 0:1], pdg[:, :], sfin[:, k:k + 1], Op.subtract),
                    [s0, op_pdf])
                s2_ = sc.add('vector', lambda v, k=k: v.tensor_tensor(
                    sc1[:, 1:2], sc1[:, 0:1], sc2[:, 3 * k + 2:3 * k + 3], Op.mult),
                    [s1_])
                s3_ = sc.add('vector', lambda v, k=k: v.tensor_tensor(
                    sfin[:, k:k + 1], sfin[:, k:k + 1], sc1[:, 1:2], Op.add), [s2_])
                s4_ = sc.add('vector', lambda v, k=k: v.tensor_copy(
                    sfu[:, k:k + 1], sfin[:, k:k + 1]),
                    [s3_, prev_sck.get(k)])
                g1 = sc.add('gpsimd', lambda g, tt=tt, k=k: g.indirect_dma_start(
                    out=gl_d[:, :],
                    out_offset=IndirectOffsetOnAxis(ap=sfu[:, k:k + 1], axis=0),
                    in_=tokid[:, tt:tt + 1], in_offset=None),
                    [s4_, dl0, op_tk], dma='DL')
                g2 = sc.add('gpsimd', lambda g, tt=tt, k=k: g.indirect_dma_start(
                    out=wl_d[:, :],
                    out_offset=IndirectOffsetOnAxis(ap=sfu[:, k:k + 1], axis=0),
                    in_=w12[:, tt, k:k + 1], in_offset=None),
                    [s4_, H['w2oL'], dl1], dma='DL')
                prev_sck[k] = g2
        H['slots'] = prev_sck[1]

        B = {
            'gu': {0: H['qmm%d' % (NT - 1)], 1: H['vmm%d' % (NT - 1)]},
            'dw': None, 'hm': None, 'silu': None, 'ds': None,
            'ysum': {}, 'tp': prev['qfev'], 'tpd': {}, 'scat': {},
            'srq': None, 'aev': None, 'acc0': None,
        }

        def expert_prefetch(e):
            he = e % 2
            il = None
            for j in range(NST):
                il = sc.add('gpsimd', lambda g, e=e, j=j, he=he: g.dma_start(
                    out=idx_sb[:, he, j:j + 1],
                    in_=gl_d[e * CAP + 128 * j:e * CAP + 128 * (j + 1), :]),
                    [H['slots'], B['scat'].get(e - 2)] if j == 0 else [], dma='DG')
                il = sc.add('gpsimd', lambda g, e=e, j=j, he=he: g.dma_start(
                    out=wcol_sb[:, he, j:j + 1],
                    in_=wl_d[e * CAP + 128 * j:e * CAP + 128 * (j + 1), :]),
                    [], dma='DG')
            H['wload%d' % e] = il
            gth = None
            for j in range(NST):
                gth = sc.add('gpsimd', lambda g, j=j, he=he: g.indirect_dma_start(
                    out=gtb[:, he * NST + j, :], out_offset=None,
                    in_=xn_dram[:, :],
                    in_offset=IndirectOffsetOnAxis(ap=idx_sb[:, he, j:j + 1], axis=0)),
                    ([il, prev.get('dnd'), dn0, B['tpd'].get(e - 2)])
                    if j == 0 else [], dma='DG')
            H['gth%d' % e] = gth

        def expert_main(e):
            he = e % 2
            xgoff = he * 1024
            gth = H['gth%d' % e]
            tpv = None
            for j in range(NST):
                tpj = None
                for c in range(CH):
                    tpj = sc.add('tensor', lambda t, j=j, c=c, he=he: t.transpose(
                        ps_tp[:, c * 128:(c + 1) * 128],
                        gtb[:, he * NST + j, 128 * c:128 * (c + 1)],
                        identb[:, :]),
                        ([gth] if j == 0 else []) + [B['tp']] if c == 0 else [])
                tpv = sc.add('scalar', lambda a, j=j, xgoff=xgoff: a.copy(
                    xn_f[:, :, xgoff + 128 * j:xgoff + 128 * (j + 1)],
                    ps_tp.ap().rearrange("p (c f) -> p c f", c=CH)), [tpj])
                B['tp'] = tpv
            B['tpd'][e] = tpv
            for fg in range(NFG):
                k = e * NFG + fg
                fgw = slice(fg * 1024, (fg + 1) * 1024)
                gdma = sc.add('gpsimd', lambda g, e=e, fgw=fgw: g.dma_start(
                    out=wb0[:, :, :],
                    in_=gw_d.ap()[e, :, fgw].rearrange("(c p) f -> p c f", p=128)),
                    [B['gu'][0]], dma='DW')
                udma = sc.add('gpsimd', lambda g, e=e, fgw=fgw: g.dma_start(
                    out=wb1[:, :, :],
                    in_=uw_d.ap()[e, :, fgw].rearrange("(c p) f -> p c f", p=128)),
                    [B['gu'][1]], dma='DW')
                ddma = sc.add('gpsimd', lambda g, e=e, fgw=fgw: g.dma_start(
                    out=wb2[:, :, :],
                    in_=dw_d.ap()[e, fgw, :].rearrange("(f p) o -> p f o", p=128)),
                    [B['dw']], dma='DW')
                gu = None
                for fc in range(NFC):
                    fcs = slice(fc * 128, (fc + 1) * 128)
                    for tg in range(NTG):
                        tgw = min(512, CAP - tg * 512)
                        gmm = None
                        for c in range(CH):
                            gdeps = []
                            if c == 0:
                                gdeps = [gdma, tpv, B['silu']]
                                if k == 0:
                                    gdeps += [prevS.get(0), prevR.get(1)]
                            gmm = sc.add('tensor', lambda t, c=c, fcs=fcs, tg=tg, tgw=tgw, xgoff=xgoff: t.matmul(
                                ps_c[:, 0:tgw], wb0[:, c, fcs],
                                xn_f[:, c, xgoff + tg * 512:xgoff + tg * 512 + tgw],
                                start=(c == 0), stop=(c == CH - 1)), gdeps)
                        umm = None
                        for c in range(CH):
                            udeps = []
                            if c == 0:
                                udeps = [udma, B['hm']]
                                if k == 0:
                                    udeps.append(sev1)
                                if k == NFG:
                                    udeps.append(B['srq'])
                            umm = sc.add('tensor', lambda t, c=c, fcs=fcs, tg=tg, tgw=tgw, xgoff=xgoff: t.matmul(
                                ps_s1[:, 0:tgw], wb1[:, c, fcs],
                                xn_f[:, c, xgoff + tg * 512:xgoff + tg * 512 + tgw],
                                start=(c == 0), stop=(c == CH - 1)), udeps)
                        sil = sc.add('scalar', lambda a, tgw=tgw: a.activation(
                            qt[:, 0:tgw], ps_c[:, 0:tgw], AF.Silu),
                            [gmm, B['hm']] + ([prevS.get(0)] if k == 0 else []))
                        B['silu'] = sil
                        hm = sc.add('vector', lambda v, fc=fc, tg=tg, tgw=tgw: v.tensor_tensor(
                            hbuf[:, fc, tg * 512:tg * 512 + tgw], qt[:, 0:tgw],
                            ps_s1[:, 0:tgw], Op.mult), [sil, umm])
                        B['hm'] = hm
                        gu = hm
                B['gu'] = {0: gu, 1: gu}
                H['gu%d' % e] = gu
                dn = None
                for st in range(NST):
                    psy = ps_q if st % 2 == 0 else ps_v
                    dmm = None
                    for hf in range(2):
                        cols = slice(hf * 512, (hf + 1) * 512)
                        for fc in range(NFC):
                            ddeps = []
                            if hf == 0 and fc == 0:
                                ddeps = [ddma, gu, B['ysum'].get(st % 2)]
                                if k == 0:
                                    ddeps += [prevR.get('P0'), prevR.get('P1'),
                                              prev.get('eq'), brv]
                                if k == NFG:
                                    ddeps += [B['aev'], B['acc0']]
                            dmm = sc.add('tensor', lambda t, fc=fc, cols=cols, st=st, psy=psy: t.matmul(
                                psy[:, cols], hbuf[:, fc, st * 128:(st + 1) * 128],
                                wb2[:, fc, cols],
                                start=(fc == 0), stop=(fc == NFC - 1)), ddeps)
                    if fg == 0:
                        yv = sc.add('vector', lambda v, st=st, psy=psy: v.tensor_scalar(
                            ysum[:, st, :], psy[:, :], 1.0, None, Op.mult),
                            [dmm, B['scat'].get('st%d' % st)])
                    else:
                        yv = sc.add('vector', lambda v, st=st, psy=psy: v.scalar_tensor_tensor(
                            ysum[:, st, :], psy[:, :], 1.0, ysum[:, st, :],
                            Op.mult, Op.add), [dmm])
                    B['ysum'][st % 2] = yv
                    H['yv%d_%d' % (k, st)] = yv
                    dn = dmm
                B['dw'] = dn
            H['dn%d' % e] = dn

        def expert_scatter(e):
            he = e % 2
            for st in range(NST):
                se = sc.add('scalar', lambda a, st=st, he=he: a.activation(
                    xt[:, st % 2, :], ysum[:, st, :], AF.Copy,
                    scale=wcol_sb[:, he, st:st + 1]),
                    [H['yv%d_%d' % (e * NFG + NFG - 1, st)], H['wload%d' % e],
                     B['scat'].get('ys%d' % (st % 2)), B['acc0']])
                sct = sc.add('gpsimd', lambda g, st=st, he=he: g.indirect_dma_start(
                    out=acc_dram[:, :],
                    out_offset=IndirectOffsetOnAxis(ap=idx_sb[:, he, st:st + 1], axis=0),
                    in_=xt[:, st % 2, :], in_offset=None,
                    compute_op=Op.add),
                    [se, B['ds']], dma='DS')
                B['ds'] = sct
                B['scat']['st%d' % st] = sct
                B['scat']['ys%d' % (st % 2)] = sct
            B['scat'][e] = sct

        expert_prefetch(0)
        expert_prefetch(1)
        expert_main(0)

        db1 = sc.add('gpsimd', lambda g: g.dma_start(
            out=csb[:, :], in_=cc_out[0:128, :]),
            [ccb, d1, H['gu0']], dma='DC')
        db2 = sc.add('gpsimd', lambda g: g.dma_start(
            out=sr8[:, :],
            in_=cc_out.ap()[128:130, :].rearrange("a (c q) -> (a c) q", q=128)),
            [ccb], dma='DC')
        stp = sc.add('tensor', lambda t: t.transpose(
            ps_s1[:, 0:8], sr8[:, :], identf[0:8, 0:8]),
            [db2, op_idf, sev1, H['gu0']])
        srq = sc.add('vector', lambda v: v.reciprocal(scol[:, :], ps_s1[:, 0:8]), [stp])
        B['srq'] = srq
        cbl = None
        for p in range(NP):
            cs = slice(64 * p, 64 * (p + 1))
            cbl = sc.add('vector', lambda v, p=p, cs=cs: v.tensor_scalar(
                e2t[:, cs], csb[:, cs], scol[:, p:p + 1], S_CONST, Op.mult, Op.mult),
                [srq, db1, prev['e2']] if p == 0 else [])
        op_wo2 = sc.add('gpsimd', lambda g: g.dma_start(
            out=wb2[:, :, :], in_=wo_d.ap().rearrange("(c p) o -> p c o", p=128)),
            [H['dn0']], dma='DW')
        xre = [None] * NT
        xre[0] = sc.add('sync', lambda s: s.dma_start(
            out=xt[:, 0, :], in_=x_d[0:128, :]), [H['qm%d' % (NT - 2)]], dma='DX')
        xre[1] = sc.add('sync', lambda s: s.dma_start(
            out=xt[:, 1, :], in_=x_d[128:256, :]), [H['qm%d' % (NT - 1)]], dma='DX')
        prev3 = {}
        for tt in range(NT):
            amm = None
            for p in range(NP):
                for j in range(2):
                    rows = slice(64 * j, 64 * (j + 1))
                    amm = sc.add('tensor', lambda t, p=p, rows=rows, tt=tt: t.matmul(
                        ps_q[rows, 128 * p:128 * (p + 1)],
                        e2t[rows, 64 * p:64 * (p + 1)],
                        qf_h.ap()[rows, p, tt * 128:(tt + 1) * 128],
                        start=True, stop=True, skip_group_check=True),
                        [cbl, prev3.get('aev'), B['ysum'].get(0), B['ysum'].get(1)]
                        if (p == 0 and j == 0) else [])
            aev = sc.add('scalar', lambda a: a.copy(vt[:, :], ps_q[:, :]),
                         [amm, prev3.get('omm'), prev.get('CS')])
            prev3['aev'] = aev
            omm = None
            for hf in range(2):
                cols = slice(hf * 512, (hf + 1) * 512)
                omm = sc.add('tensor', lambda t, cols=cols: t.matmul(
                    ps_v[:, cols], ones1[:, :], bob[:, cols], start=True, stop=False),
                    [aev, op_wo2, op_bo, B['ysum'].get(1),
                     prev3.get('acc0') or brv] if hf == 0 else [])
                for c in range(CH):
                    omm = sc.add('tensor', lambda t, c=c, cols=cols: t.matmul(
                        ps_v[:, cols], vt[:, c * 128:(c + 1) * 128], wb2[:, c, cols],
                        start=False, stop=(c == CH - 1)))
            prev3['omm'] = omm
            a0 = sc.add('vector', lambda v, sl=tt % 2: v.scalar_tensor_tensor(
                acc[:, sl, :], ps_v[:, :], 1.0, xt[:, sl, :], Op.mult, Op.add),
                [omm, xre[tt], prev3.get('ad%d' % (tt % 2))])
            prev3['acc0'] = a0
            adp = sc.add('gpsimd', lambda g, a=tt * 128, sl=tt % 2: g.dma_start(
                out=acc_dram[a:a + 128, :], in_=acc[:, sl, :]), [a0], dma='DS')
            prev3['ad%d' % (tt % 2)] = adp
            if tt + 2 < NT:
                xre[tt + 2] = sc.add('sync', lambda s, a=(tt + 2) * 128, sl=tt % 2: s.dma_start(
                    out=xt[:, sl, :], in_=x_d[a:a + 128, :]), [a0], dma='DX')
        B['aev'] = prev3['aev']
        B['acc0'] = prev3['acc0']
        B['ds'] = prev3['ad%d' % ((NT - 1) % 2)]
        B['gu'] = {0: B['gu'][0], 1: B['gu'][1]}
        B['dw'] = prev3['omm']

        expert_scatter(0)
        for e in range(1, n_experts):
            if e + 1 < n_experts:
                expert_prefetch(e + 1)
            expert_main(e)
            expert_scatter(e)
        prev_ds = B['ds']

        if tap is True:
            prev_ds = sc.add('gpsimd', lambda g: g.dma_start(
                out=gl_o[:, :], in_=gl_d[:, :]), [H['slots'], prev_ds], dma='DS')
            prev_ds = sc.add('gpsimd', lambda g: g.dma_start(
                out=wl_o[:, :], in_=wl_d[:, :]), [prev_ds], dma='DS')
            prev_ds = sc.add('gpsimd', lambda g: g.dma_start(
                out=dbg_o[0:1, :], in_=t16row[:, :]), [brv, prev_ds], dma='DS')
            prev_ds = sc.add('gpsimd', lambda g: g.dma_start(
                out=dbg_o[1:2, :], in_=brow[:, :]), [prev_ds], dma='DS')
        if tap in ('mini', True):
            prev_ds = sc.add('gpsimd', lambda g: g.dma_start(
                out=dbgs_o[:, :], in_=scol[:, :]), [srq, prev_ds], dma='DS')
            prev_ds = sc.add('gpsimd', lambda g: g.dma_start(
                out=dbgr_o[:, :], in_=sr8[:, :]), [prev_ds], dma='DS')
        if tap is True:
            prev_ds = sc.add('gpsimd', lambda g: g.dma_start(
                out=xnf_o[:, :, :], in_=xn_f[:, :, :]), [prev_ds], dma='DS')
            prev_ds = sc.add('gpsimd', lambda g: g.dma_start(
                out=qfh_o[:, :, :], in_=qf_h[:, :, :]), [prev_ds], dma='DS')
            prev_ds = sc.add('gpsimd', lambda g: g.dma_start(
                out=ysum_o[:, :, :], in_=ysum[:, :, :]), [prev_ds], dma='DS')
        fcp = None
        prev_b = {}
        for tt in range(NT):
            sl = tt % 2
            fin_in = sc.add('gpsimd', lambda g, a=tt * 128, sl=sl: g.dma_start(
                out=xt[:, sl, :], in_=acc_dram[a:a + 128, :]),
                [prev_ds, prev_b.get(sl)], dma='DS')
            fcp = sc.add('gpsimd', lambda g, a=tt * 128, sl=sl: g.dma_start(
                out=y_d[a:a + 128, :], in_=xt[:, sl, :]), [fin_in], dma='DS')
            prev_b[sl] = fcp
        sc.add('sync', None, [fcp])

        sc.finalize()
        ds_total = sum(16 for o in sc.ops if o.dma == 'DS')

        with nc.Block() as block:
            block.sync(lambda eng: (sc.emit('sync', eng, sems),
                                    eng.wait_ge(sems['DS'], ds_total)))
            block.tensor(lambda eng: sc.emit('tensor', eng, sems))
            block.vector(lambda eng: sc.emit('vector', eng, sems))
            block.scalar(lambda eng: sc.emit('scalar', eng, sems))
            block.gpsimd(lambda eng: sc.emit('gpsimd', eng, sems))

    return nc


_NC_CACHE = {}


def _get_nc():
    if 'nc' not in _NC_CACHE:
        _NC_CACHE['nc'] = build_nc(tap='mini')
    return _NC_CACHE['nc']


def kernel(L0_attn_x, L0_attn_Wq, L0_attn_bq, L0_attn_Wk, L0_attn_bk,
           L0_attn_Wv, L0_attn_bv, L0_attn_Wo, L0_attn_bo,
           L0_ffn_router_w, L0_ffn_gate_w, L0_ffn_up_w, L0_ffn_down_w):
    x = np.ascontiguousarray(np.asarray(L0_attn_x, dtype=np.float32).reshape(2 * 8192, DM))
    wq = np.ascontiguousarray(np.asarray(L0_attn_Wq, dtype=np.float32))
    bq = np.ascontiguousarray(np.asarray(L0_attn_bq, dtype=np.float32).reshape(1, DM))
    wv = np.ascontiguousarray(np.asarray(L0_attn_Wv, dtype=np.float32))
    bv = np.ascontiguousarray(np.asarray(L0_attn_bv, dtype=np.float32).reshape(1, DM))
    wo = np.ascontiguousarray(np.asarray(L0_attn_Wo, dtype=np.float32))
    bo = np.ascontiguousarray(np.asarray(L0_attn_bo, dtype=np.float32).reshape(1, DM))
    rt = np.ascontiguousarray(np.asarray(L0_ffn_router_w, dtype=np.float32).T)
    gw = np.ascontiguousarray(np.asarray(L0_ffn_gate_w, dtype=np.float32).transpose(0, 2, 1))
    uw = np.ascontiguousarray(np.asarray(L0_ffn_up_w, dtype=np.float32).transpose(0, 2, 1))
    dw = np.ascontiguousarray(np.asarray(L0_ffn_down_w, dtype=np.float32).transpose(0, 2, 1))
    tri = np.triu(np.ones((128, 128), np.float32), 1).astype(ml_dtypes.bfloat16)
    kron = np.kron(np.triu(np.ones((NT, NT), np.float32), 1),
                   np.eye(NE, dtype=np.float32)).astype(ml_dtypes.bfloat16)

    nc = _get_nc()
    in_maps = []
    for c in range(NCORES):
        in_maps.append({
            'x': np.ascontiguousarray(x[c * TTOK:(c + 1) * TTOK]),
            'wq': wq, 'bq': bq, 'wv': wv, 'bv': bv, 'wo': wo, 'bo': bo,
            'rt': rt, 'tri': tri, 'kron': kron,
            'gw': gw, 'uw': uw, 'dw': dw,
        })
    res = run_bass_kernel_spmd(nc, in_maps, core_ids=list(range(NCORES)))
    out = np.concatenate([r['y'] for r in res.results], axis=0)
    return np.ascontiguousarray(out.reshape(2, 8192, DM).astype(np.float32))


# version 11
# speedup vs baseline: 1.2404x; 1.0089x over previous
import numpy as np
import ml_dtypes
import concourse.bass as bass
import concourse.mybir as mybir
from concourse.bass_utils import run_bass_kernel_spmd
from concourse.bass import IndirectOffsetOnAxis
from concourse.alu_op_type import AluOpType as Op
from concourse.hw_specs import get_activation_tables
from contextlib import ExitStack

AF = mybir.ActivationFunctionType
AX = mybir.AxisListType
F32 = mybir.dt.float32
BF16 = mybir.dt.bfloat16
I32 = mybir.dt.int32
U32 = mybir.dt.uint32

NCORES = 8
TTOK = 2048
NT = 16
DM = 1024
CH = 8
NH = 16
DHD = 64
NP = 8
FF = 4096
NE = 8
NFG = 4
NFC = 8
CAP = 768
NST = CAP // 128
NTG = (CAP + 511) // 512
NSLOT = NE * CAP
S_CONST = float((64.0 ** 0.5) ** (-0.25))
EPS = 1e-5

ENG_SEM = {'tensor': 'ST', 'vector': 'SV', 'scalar': 'SA', 'gpsimd': 'SG'}
SELF_CHAIN = ('vector', 'scalar')


class Opr:
    __slots__ = ('eng', 'build', 'deps', 'dma', 'marked', 'semval', 'dmaval')

    def __init__(self, eng, build, deps, dma):
        self.eng = eng
        self.build = build
        self.deps = [d for d in deps if d is not None]
        self.dma = dma
        self.marked = False
        self.semval = None
        self.dmaval = None


class Sched:
    def __init__(self):
        self.ops = []
        self.last = {}

    def add(self, eng, build, deps=(), dma=None):
        o = Opr(eng, build, deps, dma)
        if eng in SELF_CHAIN and dma is None:
            p = self.last.get(eng)
            if p is not None:
                o.deps.append(p)
            self.last[eng] = o
        self.ops.append(o)
        return o

    def finalize(self):
        dma_counts = {}
        for o in self.ops:
            if o.dma is not None:
                dma_counts[o.dma] = dma_counts.get(o.dma, 0) + 1
                o.dmaval = dma_counts[o.dma] if o.dma in ('CC', 'CC2') else 16 * dma_counts[o.dma]
        for o in self.ops:
            for d in o.deps:
                if d.dma is None:
                    d.marked = True
        cum = {}
        for o in self.ops:
            if o.dma is None and o.marked:
                cum[o.eng] = cum.get(o.eng, 0) + 1
                o.semval = cum[o.eng]

    def emit(self, eng_name, eng, sems):
        last_wait = {}
        for o in self.ops:
            if o.eng != eng_name:
                continue
            for d in o.deps:
                if d.dma is not None:
                    tgt_sem, tgt_val = d.dma, d.dmaval
                else:
                    tgt_sem, tgt_val = ENG_SEM[d.eng], d.semval
                    if tgt_val is None:
                        continue
                if tgt_val > last_wait.get(tgt_sem, 0):
                    eng.wait_ge(sems[tgt_sem], tgt_val)
                    last_wait[tgt_sem] = tgt_val
            inst = o.build(eng) if o.build is not None else None
            if inst is not None:
                if o.dma in ('CC', 'CC2'):
                    inst.then_inc(sems[o.dma])
                elif o.dma is not None:
                    inst.then_inc(sems[o.dma], 16)
                elif o.marked:
                    inst.then_inc(sems[ENG_SEM[o.eng]], 1)


def build_nc(n_experts=NE, tap=False):
    nc = bass.Bass("TRN2", target_bir_lowering=False, debug=False)

    x_d = nc.dram_tensor("x", [TTOK, DM], F32, kind="ExternalInput")
    wq_d = nc.dram_tensor("wq", [DM, DM], F32, kind="ExternalInput")
    bq_d = nc.dram_tensor("bq", [1, DM], F32, kind="ExternalInput")
    wv_d = nc.dram_tensor("wv", [DM, DM], F32, kind="ExternalInput")
    bv_d = nc.dram_tensor("bv", [1, DM], F32, kind="ExternalInput")
    wo_d = nc.dram_tensor("wo", [DM, DM], F32, kind="ExternalInput")
    bo_d = nc.dram_tensor("bo", [1, DM], F32, kind="ExternalInput")
    rt_d = nc.dram_tensor("rt", [DM, NE], F32, kind="ExternalInput")
    tri_d = nc.dram_tensor("tri", [128, 128], BF16, kind="ExternalInput")
    kron_d = nc.dram_tensor("kron", [128, 128], BF16, kind="ExternalInput")
    gw_d = nc.dram_tensor("gw", [NE, DM, FF], F32, kind="ExternalInput")
    uw_d = nc.dram_tensor("uw", [NE, DM, FF], F32, kind="ExternalInput")
    dw_d = nc.dram_tensor("dw", [NE, FF, DM], F32, kind="ExternalInput")
    y_d = nc.dram_tensor("y", [TTOK, DM], F32, kind="ExternalOutput")
    cc_in = nc.dram_tensor("cc_in", [260, 512], F32)
    cc_out = nc.dram_tensor("cc_out", [260, 512], F32)
    xn_dram = nc.dram_tensor("xn_dram", [TTOK + 128, DM], BF16)
    acc_dram = nc.dram_tensor("acc_dram", [TTOK + 128, DM], F32)
    cc_junk = nc.dram_tensor("cc_junk", [32, 1024], F32)
    gl_d = nc.dram_tensor("gl", [NSLOT + 128, 1], U32)
    wl_d = nc.dram_tensor("wl", [NSLOT + 128, 1], F32)
    gl_o = wl_o = dbg_o = None
    if tap == 'mini':
        dbgs_o = nc.dram_tensor("dbgs_o", [128, 8], F32, kind="ExternalOutput")
        dbgr_o = nc.dram_tensor("dbgr_o", [8, 128], F32, kind="ExternalOutput")
    if tap is True:
        gl_o = nc.dram_tensor("gl_o", [NSLOT + 128, 1], U32, kind="ExternalOutput")
        wl_o = nc.dram_tensor("wl_o", [NSLOT + 128, 1], F32, kind="ExternalOutput")
        dbg_o = nc.dram_tensor("dbg_o", [2, 128], F32, kind="ExternalOutput")
        dbgs_o = nc.dram_tensor("dbgs_o", [128, 8], F32, kind="ExternalOutput")
        dbgr_o = nc.dram_tensor("dbgr_o", [8, 128], F32, kind="ExternalOutput")
        xnf_o = nc.dram_tensor("xnf_o", [128, CH, TTOK], BF16, kind="ExternalOutput")
        qfh_o = nc.dram_tensor("qfh_o", [128, CH, TTOK], BF16, kind="ExternalOutput")
        ysum_o = nc.dram_tensor("ysum_o", [128, NST, DM], F32, kind="ExternalOutput")

    act_tables = list(get_activation_tables(nc.m.arch).keys())
    IDX_EXPLN = act_tables.index("natural_log_exp_and_others")
    IDX_SILU = act_tables.index("silu_and_others")

    with ExitStack() as es:
        ec = es.enter_context

        xn_f = ec(nc.sbuf_tensor("xn_f", [128, CH, TTOK], BF16))
        qf_h = ec(nc.sbuf_tensor("qf_h", [128, CH, TTOK], BF16))
        acc = ec(nc.sbuf_tensor("acc", [128, 2, DM], F32))
        ysum = ec(nc.sbuf_tensor("ysum", [128, NST, DM], F32))
        hbuf = ec(nc.sbuf_tensor("hbuf", [128, CH, CAP], BF16))
        gtb = ec(nc.sbuf_tensor("gtb", [128, NST, DM], BF16))
        wb0 = ec(nc.sbuf_tensor("wb0", [128, CH, DM], BF16))
        wb1 = ec(nc.sbuf_tensor("wb1", [128, CH, DM], BF16))
        wb2 = ec(nc.sbuf_tensor("wb2", [128, CH, DM], BF16))
        rt_b = ec(nc.sbuf_tensor("rt_b", [128, CH, NE], BF16))
        bob = ec(nc.sbuf_tensor("bob", [1, DM], BF16))
        xt = ec(nc.sbuf_tensor("xt", [128, 2, DM], F32))
        qt = ec(nc.sbuf_tensor("qt", [128, DM], BF16))
        e2t = ec(nc.sbuf_tensor("e2t", [128, DM], BF16))
        vt = ec(nc.sbuf_tensor("vt", [128, DM], BF16))
        csb = ec(nc.sbuf_tensor("csb", [128, 512], F32))
        sr8 = ec(nc.sbuf_tensor("sr8", [8, 128], F32))
        scol = ec(nc.sbuf_tensor("scol", [128, 8], F32))
        bns = ec(nc.sbuf_tensor("bns", [128, 12], F32))
        mv = ec(nc.sbuf_tensor("mv", [128, 2], F32))
        lnv = ec(nc.sbuf_tensor("lnv", [128, 1], F32))
        rstd = ec(nc.sbuf_tensor("rstd", [128, 1], F32))
        ssum = ec(nc.sbuf_tensor("ssum", [128, NH], F32))
        rq = ec(nc.sbuf_tensor("rq", [128, NH], F32))
        rsb = ec(nc.sbuf_tensor("rsb", [128, NE], F32))
        msk1 = ec(nc.sbuf_tensor("msk1", [128, NE], F32))
        msk2 = ec(nc.sbuf_tensor("msk2", [128, NE], F32))
        rmsk = ec(nc.sbuf_tensor("rmsk", [128, NE], F32))
        ctm = ec(nc.sbuf_tensor("ctm", [128, NE], F32))
        m1 = ec(nc.sbuf_tensor("m1", [128, 1], F32))
        m2 = ec(nc.sbuf_tensor("m2", [128, 1], F32))
        dd = ec(nc.sbuf_tensor("dd", [128, 1], F32))
        ew = ec(nc.sbuf_tensor("ew", [128, 1], F32))
        w12 = ec(nc.sbuf_tensor("w12", [128, NT, 2], F32))
        msk1b = ec(nc.sbuf_tensor("msk1b", [128, NT, NE], BF16))
        msk2b = ec(nc.sbuf_tensor("msk2b", [128, NT, NE], BF16))
        msks = ec(nc.sbuf_tensor("msks", [128, NT, NE], BF16))
        P_sb = ec(nc.sbuf_tensor("P_sb", [128, NT, NE], F32))
        bp_sb = ec(nc.sbuf_tensor("bp_sb", [128, NE], F32))
        ovf_sb = ec(nc.sbuf_tensor("ovf_sb", [128, NE], F32))
        sc1 = ec(nc.sbuf_tensor("sc1", [128, NE], F32))
        sc2 = ec(nc.sbuf_tensor("sc2", [128, 6], F32))
        sfin = ec(nc.sbuf_tensor("sfin", [128, 2], F32))
        sfu = ec(nc.sbuf_tensor("sfu", [128, 2], U32))
        eidx = ec(nc.sbuf_tensor("eidx", [128, NE], F32))
        pdg = ec(nc.sbuf_tensor("pdg", [128, 1], F32))
        tokid = ec(nc.sbuf_tensor("tokid", [128, NT], U32))
        idx_sb = ec(nc.sbuf_tensor("idx_sb", [128, 2, NST], U32))
        wcol_sb = ec(nc.sbuf_tensor("wcol_sb", [128, 2, NST], F32))
        t16row = ec(nc.sbuf_tensor("t16row", [1, 128], F32))
        t16b = ec(nc.sbuf_tensor("t16b", [128, 1], BF16))
        bcol = ec(nc.sbuf_tensor("bcol", [128, 1], F32))
        brow = ec(nc.sbuf_tensor("brow", [1, 128], F32))
        ones1 = ec(nc.sbuf_tensor("ones1", [1, 128], BF16))
        ones1f = ec(nc.sbuf_tensor("ones1f", [1, 128], F32))
        ones128 = ec(nc.sbuf_tensor("ones128", [128, 1], BF16))
        identb = ec(nc.sbuf_tensor("identb", [128, 128], BF16))
        identf = ec(nc.sbuf_tensor("identf", [128, 128], F32))
        cst_eps = ec(nc.sbuf_tensor("cst_eps", [128, 1], F32))
        nc.const_aps.register(F32, EPS, cst_eps[:, :])

        ps_tp = ec(nc.psum_tensor("ps_tp", [128, 1024], BF16))
        ps_q = ec(nc.psum_tensor("ps_q", [128, 1024], F32))
        ps_v = ec(nc.psum_tensor("ps_v", [128, 1024], F32))
        ps_c = ec(nc.psum_tensor("ps_c", [128, 512], F32))
        ps_s1 = ec(nc.psum_tensor("ps_s1", [128, 512], F32))
        ps_s2 = ec(nc.psum_tensor("ps_s2", [128, 512], F32))

        sems = {}
        for name in ['ST', 'SV', 'SA', 'SG', 'DX', 'DW', 'DN', 'DL', 'DG',
                     'DS', 'DC', 'CC', 'CC2']:
            sems[name] = ec(nc.semaphore(name))

        sc = Sched()
        H = {}

        op_iota = sc.add('gpsimd', lambda g: g.iota(
            identf[:, :], [[1, 128]], base=0, channel_multiplier=-1,
            allow_small_or_imprecise_dtypes=True))
        op_idb = sc.add('vector', lambda v: v.tensor_scalar(
            identb[:, :], identf[:, :], 0.0, None, Op.is_equal), [op_iota])
        op_idf = sc.add('vector', lambda v: v.tensor_scalar(
            identf[:, :], identf[:, :], 0.0, None, Op.is_equal), [op_idb])
        op_o1 = sc.add('vector', lambda v: v.memset(ones1[:, :], 1.0))
        op_o1f = sc.add('vector', lambda v: v.memset(ones1f[:, :], 1.0))
        op_o128 = sc.add('vector', lambda v: v.memset(ones128[:, :], 1.0))
        op_eps = sc.add('vector', lambda v: v.memset(cst_eps[:, :], EPS))
        op_eif = sc.add('gpsimd', lambda g: g.iota(
            eidx[:, :], [[1, NE]], base=0, channel_multiplier=0,
            allow_small_or_imprecise_dtypes=True), [op_iota])
        op_pdf = sc.add('gpsimd', lambda g: g.iota(
            pdg[:, :], [[1, 1]], base=NSLOT, channel_multiplier=1,
            allow_small_or_imprecise_dtypes=True), [op_eif])
        op_tk = sc.add('gpsimd', lambda g: g.iota(
            tokid[:, :], [[128, NT]], base=0, channel_multiplier=1), [op_pdf])
        op_gd = sc.add('gpsimd', lambda g: g.iota(
            ysum.ap()[0:(NSLOT + 128) // 128, 0, 0:128], [[1, 128]], base=TTOK, channel_multiplier=0,
            allow_small_or_imprecise_dtypes=True), [op_tk])
        op_wd = sc.add('gpsimd', lambda g: g.memset(ysum[0:(NSLOT + 128) // 128, 0, 128:256], 0.0),
                       [op_gd])
        dl0 = sc.add('gpsimd', lambda g: g.dma_start(
            out=gl_d.ap().rearrange("(a b) o -> a (b o)", b=128),
            in_=ysum[0:(NSLOT + 128) // 128, 0, 0:128]), [op_gd], dma='DL')
        dl1 = sc.add('gpsimd', lambda g: g.dma_start(
            out=wl_d.ap().rearrange("(a b) o -> a (b o)", b=128),
            in_=ysum[0:(NSLOT + 128) // 128, 0, 128:256]), [op_wd], dma='DL')
        sc.add('scalar', lambda a: a.add_instruction(mybir.InstLoadActFuncSet(
            name=nc.get_next_instruction_name(), act_func_set_id=IDX_EXPLN,
            ins=[], outs=[])))

        op_z0 = sc.add('vector', lambda v: v.memset(hbuf.ap().rearrange("p a b -> p (a b)")[:, 0:1024], 0.0))
        dn0 = sc.add('sync', lambda s: s.dma_start(
            out=xn_dram[TTOK:TTOK + 128, :], in_=hbuf.ap().rearrange("p a b -> p (a b)")[:, 0:1024]), [op_z0], dma='DN')
        dn0b = sc.add('gpsimd', lambda g: g.dma_start(
            out=acc_dram[0:32, :],
            in_=hbuf.ap().rearrange("p a b -> p (a b)")[0:32, 0:1024]), [op_z0], dma='DN')
        ccw = sc.add('gpsimd', lambda g: g.collective_compute(
            "AllReduce", Op.add,
            replica_groups=[[0, 1, 2, 3], [4, 5, 6, 7]],
            ins=[acc_dram.ap()[0:32, :].opt()], outs=[cc_junk.ap().opt()]),
            [op_iota, dn0b], dma='CC')

        def wdma(dst, src_ap, deps=()):
            return sc.add('gpsimd',
                          lambda g, d=dst, s=src_ap: g.dma_start(out=d, in_=s),
                          deps, dma='DW')

        op_wq = wdma(wb0[:, :, :], wq_d.ap().rearrange("(c p) o -> p c o", p=128))
        op_wv = wdma(wb1[:, :, :], wv_d.ap().rearrange("(c p) o -> p c o", p=128))
        op_bq = wdma(gtb[0:1, 0, :], bq_d.ap())
        op_bv = wdma(gtb[0:1, 1, :], bv_d.ap())
        op_bo = wdma(bob[:, :], bo_d.ap())
        op_rt = wdma(rt_b[:, :, :], rt_d.ap().rearrange("(c p) e -> p c e", p=128))
        op_tri = wdma(gtb[:, 2, 0:128], tri_d.ap())
        op_kr = wdma(gtb[:, 3, 0:128], kron_d.ap())

        def xdma(tt, deps):
            return sc.add('sync', lambda s, a=tt * 128, sl=tt % 2: s.dma_start(
                out=xt[:, sl, :], in_=x_d[a:a + 128, :]), deps, dma='DX')

        xops = [None] * NT
        xops[0] = xdma(0, [])
        xops[1] = xdma(1, [])

        prev = {}
        for tt in range(NT):
            sl = tt % 2
            ttw = slice(tt * 128, (tt + 1) * 128)
            dx = xops[tt]
            b0 = sc.add('vector', lambda v, sl=sl: v.bn_stats(
                bns[:, 0:6], xt[:, sl, 0:512]), [dx])
            b1 = sc.add('vector', lambda v, sl=sl: v.bn_stats(
                bns[:, 6:12], xt[:, sl, 512:1024]), [dx])
            ag = sc.add('vector', lambda v: v.bn_aggr(mv[:, 0:2], bns[:, 0:12]),
                        [b1, prev.get('ln')])
            ln = sc.add('scalar', lambda a: a.activation(
                lnv[:, :], mv[:, 1:2], AF.Ln, bias=EPS), [ag, op_eps])
            prev['ln'] = ln
            rs = sc.add('scalar', lambda a: a.activation(
                rstd[:, :], lnv[:, :], AF.Exp, scale=-0.5), [ln, prev.get('xn')])
            xn = sc.add('vector', lambda v, sl=sl: v.tensor_scalar(
                hbuf.ap().rearrange("p a b -> p (a b)")[:, 0:1024], xt[:, sl, :], mv[:, 0:1], rstd[:, :], Op.subtract, Op.mult),
                [rs, prev.get('xnT'), prev.get('dnd'), dn0 if tt == 0 else None])
            prev['xn'] = xn
            dnd = sc.add('sync', lambda s, a=tt * 128: s.dma_start(
                out=xn_dram[a:a + 128, :], in_=hbuf.ap().rearrange("p a b -> p (a b)")[:, 0:1024]), [xn], dma='DN')
            prev['dnd'] = dnd
            tpl = None
            for k in range(CH):
                tpl = sc.add('tensor', lambda t, k=k: t.transpose(
                    ps_tp[:, k * 128:(k + 1) * 128], hbuf.ap().rearrange("p a b -> p (a b)")[:, k * 128:(k + 1) * 128],
                    identb[:, :]),
                    [xn, op_idb, prev.get('qfev')] if k == 0 else [])
            prev['xnT'] = tpl
            xnev = sc.add('scalar', lambda a, ttw=ttw: a.copy(
                xn_f[:, :, ttw], ps_tp.ap().rearrange("p (c f) -> p c f", c=CH)),
                [tpl])
            qmm = None
            for hf in range(2):
                cols = slice(hf * 512, (hf + 1) * 512)
                qmm = sc.add('tensor', lambda t, cols=cols: t.matmul(
                    ps_q[:, cols], ones1[:, :], gtb[0:1, 0, cols], start=True, stop=False),
                    [xnev, op_wq, op_bq, op_o1, prev.get('eq')] if hf == 0 else [])
                for c in range(CH):
                    qmm = sc.add('tensor', lambda t, c=c, cols=cols, ttw=ttw: t.matmul(
                        ps_q[:, cols], xn_f[:, c, ttw], wb0[:, c, cols],
                        start=False, stop=(c == CH - 1)))
            H['qmm%d' % tt] = qmm
            vmm = None
            for hf in range(2):
                cols = slice(hf * 512, (hf + 1) * 512)
                vmm = sc.add('tensor', lambda t, cols=cols: t.matmul(
                    ps_v[:, cols], ones1[:, :], gtb[0:1, 1, cols], start=True, stop=False),
                    [xnev, op_wv, op_bv, op_o1, prev.get('vev')] if hf == 0 else [])
                for c in range(CH):
                    vmm = sc.add('tensor', lambda t, c=c, cols=cols, ttw=ttw: t.matmul(
                        ps_v[:, cols], xn_f[:, c, ttw], wb1[:, c, cols],
                        start=False, stop=(c == CH - 1)))
            H['vmm%d' % tt] = vmm
            eq = sc.add('scalar', lambda a, sl=sl: a.activation(
                xt[:, sl, :], ps_q[:, :], AF.Exp), [qmm, xn])
            prev['eq'] = eq
            rd = sc.add('vector', lambda v, sl=sl: v.tensor_reduce(
                ssum[:, :], xt.ap()[:, sl, :].rearrange("p (h d) -> p h d", h=NH),
                AX.X, Op.add), [eq])
            rc = sc.add('vector', lambda v: v.reciprocal(rq[:, :], ssum[:, :]), [rd])
            qm = None
            for hh in range(NH):
                cs = slice(hh * DHD, (hh + 1) * DHD)
                qm = sc.add('vector', lambda v, cs=cs, hh=hh, sl=sl: v.tensor_scalar(
                    qt[:, cs], xt[:, sl, cs], rq[:, hh:hh + 1], S_CONST, Op.mult, Op.mult),
                    [rc, prev.get('e2'), prev.get('qT')] if hh == 0 else [])
            H['qm%d' % tt] = qm
            e2 = sc.add('scalar', lambda a: a.activation(
                e2t[:, :], qt[:, :], AF.Exp), [qm, prev.get('CS')])
            prev['e2'] = e2
            tpl2 = None
            for k in range(CH):
                tpl2 = sc.add('tensor', lambda t, k=k: t.transpose(
                    ps_tp[:, k * 128:(k + 1) * 128], qt[:, k * 128:(k + 1) * 128],
                    identb[:, :]), [qm, xnev] if k == 0 else [])
            prev['qT'] = tpl2
            qfev = sc.add('scalar', lambda a, ttw=ttw: a.copy(
                qf_h[:, :, ttw], ps_tp.ap().rearrange("p (c f) -> p c f", c=CH)),
                [tpl2])
            prev['qfev'] = qfev
            vev = sc.add('scalar', lambda a: a.copy(vt[:, :], ps_v[:, :]),
                         [vmm, prev.get('CS')])
            prev['vev'] = vev
            cmm = None
            for p in range(NP):
                for j in range(2):
                    hh = 2 * p + j
                    first = (tt == 0 and p == 0 and j == 0)
                    stopc = (tt == NT - 1 and p == NP - 1 and j == 1)
                    cmm = sc.add('tensor', lambda t, p=p, j=j, hh=hh, first=first, stopc=stopc: t.matmul(
                        ps_c[64 * j:64 * (j + 1), 64 * p:64 * (p + 1)],
                        e2t[:, hh * 64:(hh + 1) * 64], vt[:, hh * 64:(hh + 1) * 64],
                        start=first, stop=stopc, skip_group_check=True),
                        [e2, vev] if (p == 0 and j == 0) else [])
            H['Cmm%d' % tt] = cmm
            smm = sc.add('tensor', lambda t, st=(tt == 0), sp=(tt == NT - 1): t.matmul(
                ps_s1[0:1, :], ones128[:, :], e2t[:, 0:512],
                start=st, stop=sp, skip_group_check=True),
                [e2, op_o128] if tt == 0 else [e2])
            smm2 = sc.add('tensor', lambda t, st=(tt == 0), sp=(tt == NT - 1): t.matmul(
                ps_s2[0:1, :], ones128[:, :], e2t[:, 512:1024],
                start=st, stop=sp, skip_group_check=True), [])
            prev['CS'] = smm2
            H['Smm%d' % tt] = smm2
            if tt + 2 < NT:
                xops[tt + 2] = xdma(tt + 2, [qm])

        cev = sc.add('vector', lambda v: v.tensor_copy(csb[:, :], ps_c[:, :]),
                     [H['Cmm%d' % (NT - 1)]])
        sev1 = sc.add('scalar', lambda a: a.copy(acc[0:1, 0, 0:512], ps_s1[0:1, :]),
                      [H['Smm%d' % (NT - 1)]])
        sev2 = sc.add('scalar', lambda a: a.copy(acc[0:1, 0, 512:1024], ps_s2[0:1, :]),
                      [H['Smm%d' % (NT - 1)]])
        d1 = sc.add('gpsimd', lambda g: g.dma_start(out=cc_in[0:128, :], in_=csb[:, :]),
                    [cev], dma='DC')
        d1b = sc.add('gpsimd', lambda g: g.dma_start(out=cc_in[130:258, :], in_=csb[:, :]),
                     [cev], dma='DC')
        d1c = sc.add('gpsimd', lambda g: g.dma_start(out=cc_in[258:260, :], in_=csb[0:2, :]),
                     [cev], dma='DC')
        d2 = sc.add('gpsimd', lambda g: g.dma_start(out=cc_in[128:129, :], in_=acc[0:1, 0, 0:512]),
                    [sev1], dma='DC')
        d3 = sc.add('gpsimd', lambda g: g.dma_start(out=cc_in[129:130, :], in_=acc[0:1, 0, 512:1024]),
                    [sev2], dma='DC')
        ccop = sc.add('gpsimd', lambda g: g.collective_compute(
            "AllReduce", Op.add,
            replica_groups=[[0, 1, 2, 3], [4, 5, 6, 7]],
            ins=[cc_in.ap().opt()], outs=[cc_out.ap().opt()]),
            [d1, d1b, d1c, d2, d3, ccw], dma='CC2')
        ccb0 = sc.add('gpsimd', lambda g: g.collective_compute(
            "AllReduce", Op.add,
            replica_groups=[[0, 1, 2, 3], [4, 5, 6, 7]],
            ins=[cc_in.ap().opt()], outs=[cc_out.ap().opt()]),
            [ccop], dma='CC')
        ccb = sc.add('gpsimd', lambda g: g.collective_compute(
            "AllReduce", Op.add,
            replica_groups=[[0, 1, 2, 3], [4, 5, 6, 7]],
            ins=[cc_in.ap().opt()], outs=[cc_out.ap().opt()]),
            [ccb0], dma='CC2')

        prevR = {}
        ex = None
        tmm = None
        for tt in range(NT):
            ttw = slice(tt * 128, (tt + 1) * 128)
            psR = ps_s2 if tt % 2 == 0 else ps_c
            rmm = None
            for c in range(CH):
                rmm = sc.add('tensor', lambda t, c=c, ttw=ttw, psR=psR: t.matmul(
                    psR[:, 0:NE], xn_f[:, c, ttw], rt_b[:, c, :],
                    start=(c == 0), stop=(c == CH - 1)),
                    ([sev2, cev, op_rt, prevR.get(tt % 2)]) if c == 0 else [])
            rcp = sc.add('vector', lambda v, psR=psR: v.tensor_copy(
                rsb[:, :], psR[:, 0:NE]), [rmm])
            prevR[tt % 2] = rcp
            mx1 = sc.add('vector', lambda v: v.tensor_reduce(
                m1[:, :], rsb[:, :], AX.X, Op.max), [rcp])
            mk1 = sc.add('vector', lambda v: v.tensor_scalar(
                msk1[:, :], rsb[:, :], m1[:, 0:1], None, Op.is_ge), [mx1])
            rmv = sc.add('vector', lambda v: v.scalar_tensor_tensor(
                rmsk[:, :], msk1[:, :], -1e30, rsb[:, :], Op.mult, Op.add), [mk1])
            mx2 = sc.add('vector', lambda v: v.tensor_reduce(
                m2[:, :], rmsk[:, :], AX.X, Op.max), [rmv])
            sub = sc.add('vector', lambda v: v.tensor_tensor(
                dd[:, :], m1[:, :], m2[:, :], Op.subtract), [mx2])
            ex = sc.add('scalar', lambda a: a.activation(
                ew[:, :], dd[:, :], AF.Exp, scale=-1.0), [sub])
            ad_ = sc.add('vector', lambda v: v.tensor_scalar(
                ctm[:, 0:1], ew[:, :], 1.0, None, Op.add), [ex])
            rc1 = sc.add('vector', lambda v, tt=tt: v.reciprocal(
                w12[:, tt, 0:1], ctm[:, 0:1]), [ad_])
            w2o = sc.add('vector', lambda v, tt=tt: v.tensor_scalar(
                w12[:, tt, 1:2], w12[:, tt, 0:1], -1.0, 1.0, Op.mult, Op.add), [rc1])
            mk2 = sc.add('vector', lambda v: v.tensor_scalar(
                msk2[:, :], rmsk[:, :], m2[:, 0:1], None, Op.is_ge), [mx2])
            sc.add('vector', lambda v, tt=tt: v.tensor_copy(
                msk1b[:, tt, :], msk1[:, :]), [mk1])
            sc.add('vector', lambda v, tt=tt: v.tensor_copy(
                msk2b[:, tt, :], msk2[:, :]), [mk2])
            cpm = sc.add('vector', lambda v, tt=tt: v.tensor_tensor(
                msks[:, tt, :], msk1[:, :], msk2[:, :], Op.add), [mk1, mk2])
            psPh = ps_q[:, 0:NE] if tt % 2 == 0 else ps_q[:, 512:512 + NE]
            pmm = sc.add('tensor', lambda t, tt=tt, psPh=psPh: t.matmul(
                psPh, gtb[:, 2, 0:128], msks[:, tt, :], start=True, stop=True,
                skip_group_check=True),
                [cpm, op_tri] + ([prev.get('eq')] if tt < 2 else [prevR.get('P%d' % (tt % 2))]))
            pev = sc.add('vector', lambda v, tt=tt, psPh=psPh: v.tensor_copy(
                P_sb[:, tt, :], psPh), [pmm])
            prevR['P%d' % (tt % 2)] = pev
            prevR['cpm'] = cpm
        H['w2oL'] = w2o
        tmm = sc.add('tensor', lambda t: t.matmul(
            ps_v[0:1, 0:128], ones128[:, :],
            msks.ap().rearrange("p a e -> p (a e)"),
            start=True, stop=True, skip_group_check=True),
            [prevR['cpm'], prev.get('vev'), op_o128])
        sc.add('scalar', lambda a: a.add_instruction(mybir.InstLoadActFuncSet(
            name=nc.get_next_instruction_name(), act_func_set_id=IDX_SILU,
            ins=[], outs=[])), [ex])

        tev = sc.add('scalar', lambda a: a.copy(t16row[:, :], ps_v[0:1, 0:128]),
                     [tmm])
        ttr = sc.add('tensor', lambda t: t.transpose(
            ps_v[:, 128:129], t16row[:, :], identf[0:1, 0:1]), [tev, op_idf])
        tbc = sc.add('vector', lambda v: v.tensor_copy(t16b[:, :], ps_v[:, 128:129]),
                     [ttr])
        kmm = sc.add('tensor', lambda t: t.matmul(
            ps_v[:, 132:133], gtb[:, 3, 0:128], t16b[:, :], start=True, stop=True,
            skip_group_check=True), [tbc, op_kr])
        bev = sc.add('vector', lambda v: v.tensor_copy(bcol[:, :], ps_v[:, 132:133]),
                     [kmm])
        btr = sc.add('tensor', lambda t: t.transpose(
            ps_v[0:1, 136:264], bcol[:, :], identf[:, :]), [bev])
        brv = sc.add('scalar', lambda a: a.copy(brow[:, :], ps_v[0:1, 136:264]),
                     [btr])

        prevS = {}
        prev_sck = {}
        for tt in range(NT):
            psS = ps_c[:, 0:NE] if tt % 2 == 0 else ps_s2[:, 0:NE]
            bmm = sc.add('tensor', lambda t, tt=tt, psS=psS: t.matmul(
                psS, ones1f[:, :], brow[0:1, 8 * tt:8 * tt + 8],
                start=True, stop=True, skip_group_check=True),
                [brv, op_o1f, prevS.get(tt % 2),
                 prevR.get((tt % 2) ^ 1) if tt < 2 else None])
            bpo = sc.add('vector', lambda v, tt=tt, psS=psS: v.tensor_tensor(
                bp_sb[:, :], P_sb[:, tt, :], psS, Op.add), [bmm])
            prevS[tt % 2] = bpo
            ov = sc.add('vector', lambda v: v.tensor_scalar(
                ovf_sb[:, :], bp_sb[:, :], CAP - 0.5, None, Op.is_ge), [bpo])
            for k, mkb in ((0, msk1b), (1, msk2b)):
                t1 = sc.add('vector', lambda v, tt=tt, mkb=mkb: v.tensor_tensor(
                    sc1[:, :], mkb[:, tt, :], bp_sb[:, :], Op.mult), [bpo, ov])
                r1 = sc.add('vector', lambda v, k=k: v.tensor_reduce(
                    sc2[:, 3 * k:3 * k + 1], sc1[:, :], AX.X, Op.add), [t1])
                t2 = sc.add('vector', lambda v, tt=tt, mkb=mkb: v.tensor_tensor(
                    sc1[:, :], mkb[:, tt, :], eidx[:, :], Op.mult), [r1, op_eif])
                r2 = sc.add('vector', lambda v, k=k: v.tensor_reduce(
                    sc2[:, 3 * k + 1:3 * k + 2], sc1[:, :], AX.X, Op.add), [t2])
                t3 = sc.add('vector', lambda v, tt=tt, mkb=mkb: v.tensor_tensor(
                    sc1[:, :], mkb[:, tt, :], ovf_sb[:, :], Op.mult), [r2])
                r3 = sc.add('vector', lambda v, k=k: v.tensor_reduce(
                    sc2[:, 3 * k + 2:3 * k + 3], sc1[:, :], AX.X, Op.add), [t3])
                s0 = sc.add('vector', lambda v, k=k: v.scalar_tensor_tensor(
                    sfin[:, k:k + 1], sc2[:, 3 * k + 1:3 * k + 2], float(CAP),
                    sc2[:, 3 * k:3 * k + 1], Op.mult, Op.add), [r3])
                s1_ = sc.add('vector', lambda v, k=k: v.tensor_tensor(
                    sc1[:, 0:1], pdg[:, :], sfin[:, k:k + 1], Op.subtract),
                    [s0, op_pdf])
                s2_ = sc.add('vector', lambda v, k=k: v.tensor_tensor(
                    sc1[:, 1:2], sc1[:, 0:1], sc2[:, 3 * k + 2:3 * k + 3], Op.mult),
                    [s1_])
                s3_ = sc.add('vector', lambda v, k=k: v.tensor_tensor(
                    sfin[:, k:k + 1], sfin[:, k:k + 1], sc1[:, 1:2], Op.add), [s2_])
                s4_ = sc.add('vector', lambda v, k=k: v.tensor_copy(
                    sfu[:, k:k + 1], sfin[:, k:k + 1]),
                    [s3_, prev_sck.get(k)])
                g1 = sc.add('gpsimd', lambda g, tt=tt, k=k: g.indirect_dma_start(
                    out=gl_d[:, :],
                    out_offset=IndirectOffsetOnAxis(ap=sfu[:, k:k + 1], axis=0),
                    in_=tokid[:, tt:tt + 1], in_offset=None),
                    [s4_, dl0, op_tk], dma='DL')
                g2 = sc.add('gpsimd', lambda g, tt=tt, k=k: g.indirect_dma_start(
                    out=wl_d[:, :],
                    out_offset=IndirectOffsetOnAxis(ap=sfu[:, k:k + 1], axis=0),
                    in_=w12[:, tt, k:k + 1], in_offset=None),
                    [s4_, H['w2oL'], dl1], dma='DL')
                prev_sck[k] = g2
        H['slots'] = prev_sck[1]

        B = {
            'gu': {0: H['qmm%d' % (NT - 1)], 1: H['vmm%d' % (NT - 1)]},
            'dw': None, 'hm': None, 'silu': None, 'ds': None,
            'ysum': {}, 'tp': prev['qfev'], 'tpd': {}, 'scat': {},
            'srq': None, 'aev': None, 'acc0': None,
        }

        def expert_compute(e):
            he = e % 2
            xgoff = he * 1024
            il = None
            for j in range(NST):
                il = sc.add('gpsimd', lambda g, e=e, j=j, he=he: g.dma_start(
                    out=idx_sb[:, he, j:j + 1],
                    in_=gl_d[e * CAP + 128 * j:e * CAP + 128 * (j + 1), :]),
                    [H['slots'], B['scat'].get(e - 2)] if j == 0 else [], dma='DG')
                il = sc.add('gpsimd', lambda g, e=e, j=j, he=he: g.dma_start(
                    out=wcol_sb[:, he, j:j + 1],
                    in_=wl_d[e * CAP + 128 * j:e * CAP + 128 * (j + 1), :]),
                    [], dma='DG')
            H['wload%d' % e] = il
            gth = None
            for j in range(NST):
                gth = sc.add('gpsimd', lambda g, j=j, he=he: g.indirect_dma_start(
                    out=gtb[:, j, :], out_offset=None,
                    in_=xn_dram[:, :],
                    in_offset=IndirectOffsetOnAxis(ap=idx_sb[:, he, j:j + 1], axis=0)),
                    ([il, prev.get('dnd'), dn0, B['tpd'].get(e - 1)])
                    if j == 0 else [], dma='DG')
            tpv = None
            for j in range(NST):
                tpj = None
                for c in range(CH):
                    tpj = sc.add('tensor', lambda t, j=j, c=c: t.transpose(
                        ps_tp[:, c * 128:(c + 1) * 128],
                        gtb[:, j, 128 * c:128 * (c + 1)],
                        identb[:, :]),
                        ([gth] if j == 0 else []) + [B['tp']] if c == 0 else [])
                tpv = sc.add('scalar', lambda a, j=j, xgoff=xgoff: a.copy(
                    xn_f[:, :, xgoff + 128 * j:xgoff + 128 * (j + 1)],
                    ps_tp.ap().rearrange("p (c f) -> p c f", c=CH)), [tpj])
                B['tp'] = tpv
            B['tpd'][e] = tpv
            for fg in range(NFG):
                k = e * NFG + fg
                fgw = slice(fg * 1024, (fg + 1) * 1024)
                gdma = sc.add('gpsimd', lambda g, e=e, fgw=fgw: g.dma_start(
                    out=wb0[:, :, :],
                    in_=gw_d.ap()[e, :, fgw].rearrange("(c p) f -> p c f", p=128)),
                    [B['gu'][0]], dma='DW')
                udma = sc.add('gpsimd', lambda g, e=e, fgw=fgw: g.dma_start(
                    out=wb1[:, :, :],
                    in_=uw_d.ap()[e, :, fgw].rearrange("(c p) f -> p c f", p=128)),
                    [B['gu'][1]], dma='DW')
                ddma = sc.add('gpsimd', lambda g, e=e, fgw=fgw: g.dma_start(
                    out=wb2[:, :, :],
                    in_=dw_d.ap()[e, fgw, :].rearrange("(f p) o -> p f o", p=128)),
                    [B['dw']], dma='DW')
                gu = None
                for fc in range(NFC):
                    fcs = slice(fc * 128, (fc + 1) * 128)
                    for tg in range(NTG):
                        tgw = min(512, CAP - tg * 512)
                        gmm = None
                        for c in range(CH):
                            gdeps = []
                            if c == 0:
                                gdeps = [gdma, tpv, B['silu']]
                                if k == 0:
                                    gdeps += [prevS.get(0), prevR.get(1)]
                            gmm = sc.add('tensor', lambda t, c=c, fcs=fcs, tg=tg, tgw=tgw, xgoff=xgoff: t.matmul(
                                ps_c[:, 0:tgw], wb0[:, c, fcs],
                                xn_f[:, c, xgoff + tg * 512:xgoff + tg * 512 + tgw],
                                start=(c == 0), stop=(c == CH - 1)), gdeps)
                        umm = None
                        for c in range(CH):
                            udeps = []
                            if c == 0:
                                udeps = [udma, B['hm']]
                                if k == 0:
                                    udeps.append(sev1)
                                if k == NFG:
                                    udeps.append(B['srq'])
                            umm = sc.add('tensor', lambda t, c=c, fcs=fcs, tg=tg, tgw=tgw, xgoff=xgoff: t.matmul(
                                ps_s1[:, 0:tgw], wb1[:, c, fcs],
                                xn_f[:, c, xgoff + tg * 512:xgoff + tg * 512 + tgw],
                                start=(c == 0), stop=(c == CH - 1)), udeps)
                        sil = sc.add('scalar', lambda a, tgw=tgw: a.activation(
                            qt[:, 0:tgw], ps_c[:, 0:tgw], AF.Silu),
                            [gmm, B['hm']] + ([prevS.get(0)] if k == 0 else []))
                        B['silu'] = sil
                        hm = sc.add('vector', lambda v, fc=fc, tg=tg, tgw=tgw: v.tensor_tensor(
                            hbuf[:, fc, tg * 512:tg * 512 + tgw], qt[:, 0:tgw],
                            ps_s1[:, 0:tgw], Op.mult), [sil, umm])
                        B['hm'] = hm
                        gu = hm
                B['gu'] = {0: gu, 1: gu}
                H['gu%d' % e] = gu
                dn = None
                for st in range(NST):
                    psy = ps_q if st % 2 == 0 else ps_v
                    dmm = None
                    for hf in range(2):
                        cols = slice(hf * 512, (hf + 1) * 512)
                        for fc in range(NFC):
                            ddeps = []
                            if hf == 0 and fc == 0:
                                ddeps = [ddma, gu, B['ysum'].get(st % 2)]
                                if k == 0:
                                    ddeps += [prevR.get('P0'), prevR.get('P1'),
                                              prev.get('eq'), brv]
                                if k == NFG:
                                    ddeps += [B['aev'], B['acc0']]
                            dmm = sc.add('tensor', lambda t, fc=fc, cols=cols, st=st, psy=psy: t.matmul(
                                psy[:, cols], hbuf[:, fc, st * 128:(st + 1) * 128],
                                wb2[:, fc, cols],
                                start=(fc == 0), stop=(fc == NFC - 1)), ddeps)
                    if fg == 0:
                        yv = sc.add('vector', lambda v, st=st, psy=psy: v.tensor_scalar(
                            ysum[:, st, :], psy[:, :], 1.0, None, Op.mult),
                            [dmm, B['scat'].get('st%d' % st)])
                    else:
                        yv = sc.add('vector', lambda v, st=st, psy=psy: v.scalar_tensor_tensor(
                            ysum[:, st, :], psy[:, :], 1.0, ysum[:, st, :],
                            Op.mult, Op.add), [dmm])
                    B['ysum'][st % 2] = yv
                    H['yv%d_%d' % (k, st)] = yv
                    dn = dmm
                B['dw'] = dn
            H['dn%d' % e] = dn

        def expert_scatter(e):
            he = e % 2
            for st in range(NST):
                se = sc.add('scalar', lambda a, st=st, he=he: a.activation(
                    xt[:, st % 2, :], ysum[:, st, :], AF.Copy,
                    scale=wcol_sb[:, he, st:st + 1]),
                    [H['yv%d_%d' % (e * NFG + NFG - 1, st)], H['wload%d' % e],
                     B['scat'].get('ys%d' % (st % 2)), B['acc0']])
                sct = sc.add('gpsimd', lambda g, st=st, he=he: g.indirect_dma_start(
                    out=acc_dram[:, :],
                    out_offset=IndirectOffsetOnAxis(ap=idx_sb[:, he, st:st + 1], axis=0),
                    in_=xt[:, st % 2, :], in_offset=None,
                    compute_op=Op.add),
                    [se, B['ds']], dma='DS')
                B['ds'] = sct
                B['scat']['st%d' % st] = sct
                B['scat']['ys%d' % (st % 2)] = sct
            B['scat'][e] = sct

        expert_compute(0)

        db1 = sc.add('gpsimd', lambda g: g.dma_start(
            out=csb[:, :], in_=cc_out[0:128, :]),
            [ccb, d1, H['gu0']], dma='DC')
        db2 = sc.add('gpsimd', lambda g: g.dma_start(
            out=sr8[:, :],
            in_=cc_out.ap()[128:130, :].rearrange("a (c q) -> (a c) q", q=128)),
            [ccb], dma='DC')
        stp = sc.add('tensor', lambda t: t.transpose(
            ps_s1[:, 0:8], sr8[:, :], identf[0:8, 0:8]),
            [db2, op_idf, sev1, H['gu0']])
        srq = sc.add('vector', lambda v: v.reciprocal(scol[:, :], ps_s1[:, 0:8]), [stp])
        B['srq'] = srq
        cbl = None
        for p in range(NP):
            cs = slice(64 * p, 64 * (p + 1))
            cbl = sc.add('vector', lambda v, p=p, cs=cs: v.tensor_scalar(
                e2t[:, cs], csb[:, cs], scol[:, p:p + 1], S_CONST, Op.mult, Op.mult),
                [srq, db1, prev['e2']] if p == 0 else [])
        op_wo2 = sc.add('gpsimd', lambda g: g.dma_start(
            out=wb2[:, :, :], in_=wo_d.ap().rearrange("(c p) o -> p c o", p=128)),
            [H['dn0']], dma='DW')
        xre = [None] * NT
        xre[0] = sc.add('sync', lambda s: s.dma_start(
            out=xt[:, 0, :], in_=x_d[0:128, :]), [H['qm%d' % (NT - 2)]], dma='DX')
        xre[1] = sc.add('sync', lambda s: s.dma_start(
            out=xt[:, 1, :], in_=x_d[128:256, :]), [H['qm%d' % (NT - 1)]], dma='DX')
        prev3 = {}
        for tt in range(NT):
            amm = None
            for p in range(NP):
                for j in range(2):
                    rows = slice(64 * j, 64 * (j + 1))
                    amm = sc.add('tensor', lambda t, p=p, rows=rows, tt=tt: t.matmul(
                        ps_q[rows, 128 * p:128 * (p + 1)],
                        e2t[rows, 64 * p:64 * (p + 1)],
                        qf_h.ap()[rows, p, tt * 128:(tt + 1) * 128],
                        start=True, stop=True, skip_group_check=True),
                        [cbl, prev3.get('aev'), B['ysum'].get(0), B['ysum'].get(1)]
                        if (p == 0 and j == 0) else [])
            aev = sc.add('scalar', lambda a: a.copy(vt[:, :], ps_q[:, :]),
                         [amm, prev3.get('omm'), prev.get('CS')])
            prev3['aev'] = aev
            omm = None
            for hf in range(2):
                cols = slice(hf * 512, (hf + 1) * 512)
                omm = sc.add('tensor', lambda t, cols=cols: t.matmul(
                    ps_v[:, cols], ones1[:, :], bob[:, cols], start=True, stop=False),
                    [aev, op_wo2, op_bo, B['ysum'].get(1),
                     prev3.get('acc0') or brv] if hf == 0 else [])
                for c in range(CH):
                    omm = sc.add('tensor', lambda t, c=c, cols=cols: t.matmul(
                        ps_v[:, cols], vt[:, c * 128:(c + 1) * 128], wb2[:, c, cols],
                        start=False, stop=(c == CH - 1)))
            prev3['omm'] = omm
            a0 = sc.add('vector', lambda v, sl=tt % 2: v.scalar_tensor_tensor(
                acc[:, sl, :], ps_v[:, :], 1.0, xt[:, sl, :], Op.mult, Op.add),
                [omm, xre[tt], prev3.get('ad%d' % (tt % 2))])
            prev3['acc0'] = a0
            adp = sc.add('gpsimd', lambda g, a=tt * 128, sl=tt % 2: g.dma_start(
                out=acc_dram[a:a + 128, :], in_=acc[:, sl, :]), [a0], dma='DS')
            prev3['ad%d' % (tt % 2)] = adp
            if tt + 2 < NT:
                xre[tt + 2] = sc.add('sync', lambda s, a=(tt + 2) * 128, sl=tt % 2: s.dma_start(
                    out=xt[:, sl, :], in_=x_d[a:a + 128, :]), [a0], dma='DX')
        B['aev'] = prev3['aev']
        B['acc0'] = prev3['acc0']
        B['ds'] = prev3['ad%d' % ((NT - 1) % 2)]
        B['gu'] = {0: B['gu'][0], 1: B['gu'][1]}
        B['dw'] = prev3['omm']

        expert_scatter(0)
        for e in range(1, n_experts):
            expert_compute(e)
            expert_scatter(e)
        prev_ds = B['ds']

        if tap is True:
            prev_ds = sc.add('gpsimd', lambda g: g.dma_start(
                out=gl_o[:, :], in_=gl_d[:, :]), [H['slots'], prev_ds], dma='DS')
            prev_ds = sc.add('gpsimd', lambda g: g.dma_start(
                out=wl_o[:, :], in_=wl_d[:, :]), [prev_ds], dma='DS')
            prev_ds = sc.add('gpsimd', lambda g: g.dma_start(
                out=dbg_o[0:1, :], in_=t16row[:, :]), [brv, prev_ds], dma='DS')
            prev_ds = sc.add('gpsimd', lambda g: g.dma_start(
                out=dbg_o[1:2, :], in_=brow[:, :]), [prev_ds], dma='DS')
        if tap in ('mini', True):
            prev_ds = sc.add('gpsimd', lambda g: g.dma_start(
                out=dbgs_o[:, :], in_=scol[:, :]), [srq, prev_ds], dma='DS')
            prev_ds = sc.add('gpsimd', lambda g: g.dma_start(
                out=dbgr_o[:, :], in_=sr8[:, :]), [prev_ds], dma='DS')
        if tap is True:
            prev_ds = sc.add('gpsimd', lambda g: g.dma_start(
                out=xnf_o[:, :, :], in_=xn_f[:, :, :]), [prev_ds], dma='DS')
            prev_ds = sc.add('gpsimd', lambda g: g.dma_start(
                out=qfh_o[:, :, :], in_=qf_h[:, :, :]), [prev_ds], dma='DS')
            prev_ds = sc.add('gpsimd', lambda g: g.dma_start(
                out=ysum_o[:, :, :], in_=ysum[:, :, :]), [prev_ds], dma='DS')
        fcp = None
        prev_b = {}
        for tt in range(NT):
            sl = tt % 2
            fin_in = sc.add('gpsimd', lambda g, a=tt * 128, sl=sl: g.dma_start(
                out=xt[:, sl, :], in_=acc_dram[a:a + 128, :]),
                [prev_ds, prev_b.get(sl)], dma='DS')
            fcp = sc.add('gpsimd', lambda g, a=tt * 128, sl=sl: g.dma_start(
                out=y_d[a:a + 128, :], in_=xt[:, sl, :]), [fin_in], dma='DS')
            prev_b[sl] = fcp
        sc.add('sync', None, [fcp])

        sc.finalize()
        ds_total = sum(16 for o in sc.ops if o.dma == 'DS')

        with nc.Block() as block:
            block.sync(lambda eng: (sc.emit('sync', eng, sems),
                                    eng.wait_ge(sems['DS'], ds_total)))
            block.tensor(lambda eng: sc.emit('tensor', eng, sems))
            block.vector(lambda eng: sc.emit('vector', eng, sems))
            block.scalar(lambda eng: sc.emit('scalar', eng, sems))
            block.gpsimd(lambda eng: sc.emit('gpsimd', eng, sems))

    return nc


_NC_CACHE = {}


def _get_nc():
    if 'nc' not in _NC_CACHE:
        _NC_CACHE['nc'] = build_nc(tap='mini')
    return _NC_CACHE['nc']


def kernel(L0_attn_x, L0_attn_Wq, L0_attn_bq, L0_attn_Wk, L0_attn_bk,
           L0_attn_Wv, L0_attn_bv, L0_attn_Wo, L0_attn_bo,
           L0_ffn_router_w, L0_ffn_gate_w, L0_ffn_up_w, L0_ffn_down_w):
    x = np.ascontiguousarray(np.asarray(L0_attn_x, dtype=np.float32).reshape(2 * 8192, DM))
    wq = np.ascontiguousarray(np.asarray(L0_attn_Wq, dtype=np.float32))
    bq = np.ascontiguousarray(np.asarray(L0_attn_bq, dtype=np.float32).reshape(1, DM))
    wv = np.ascontiguousarray(np.asarray(L0_attn_Wv, dtype=np.float32))
    bv = np.ascontiguousarray(np.asarray(L0_attn_bv, dtype=np.float32).reshape(1, DM))
    wo = np.ascontiguousarray(np.asarray(L0_attn_Wo, dtype=np.float32))
    bo = np.ascontiguousarray(np.asarray(L0_attn_bo, dtype=np.float32).reshape(1, DM))
    rt = np.ascontiguousarray(np.asarray(L0_ffn_router_w, dtype=np.float32).T)
    gw = np.ascontiguousarray(np.asarray(L0_ffn_gate_w, dtype=np.float32).transpose(0, 2, 1))
    uw = np.ascontiguousarray(np.asarray(L0_ffn_up_w, dtype=np.float32).transpose(0, 2, 1))
    dw = np.ascontiguousarray(np.asarray(L0_ffn_down_w, dtype=np.float32).transpose(0, 2, 1))
    tri = np.triu(np.ones((128, 128), np.float32), 1).astype(ml_dtypes.bfloat16)
    kron = np.kron(np.triu(np.ones((NT, NT), np.float32), 1),
                   np.eye(NE, dtype=np.float32)).astype(ml_dtypes.bfloat16)

    nc = _get_nc()
    in_maps = []
    for c in range(NCORES):
        in_maps.append({
            'x': np.ascontiguousarray(x[c * TTOK:(c + 1) * TTOK]),
            'wq': wq, 'bq': bq, 'wv': wv, 'bv': bv, 'wo': wo, 'bo': bo,
            'rt': rt, 'tri': tri, 'kron': kron,
            'gw': gw, 'uw': uw, 'dw': dw,
        })
    res = run_bass_kernel_spmd(nc, in_maps, core_ids=list(range(NCORES)))
    out = np.concatenate([r['y'] for r in res.results], axis=0)
    return np.ascontiguousarray(out.reshape(2, 8192, DM).astype(np.float32))
